# version 01
# baseline (speedup 1.0000x reference)
import math
import numpy as np
import ml_dtypes
import concourse.bass as bass
import concourse.mybir as mybir
from concourse.bass_utils import run_bass_kernel_spmd

F32 = mybir.dt.float32
BF16 = mybir.dt.bfloat16
I8 = mybir.dt.int8
I32 = mybir.dt.int32
AF = mybir.ActivationFunctionType
ALU = mybir.AluOpType
AX = mybir.AxisListType

T, D, HN, DH, M, MM, FI = 4096, 1024, 16, 64, 256, 512, 2816
TTILE, NT, NSUB = 512, 8, 4
DCH, FCH, ICH, HCH = 8, 8, 22, 8
INV2PI = float(1.0 / (2 * math.pi))
NEG2PI = float(-2 * math.pi)
PIHALF = float(math.pi / 2)
PI_HI = float(np.float32(math.pi))
PI_LO = 3.1415925
NLN16 = float(-math.log(16.0))
LNEPS = 1e-5
INV127 = float(1.0 / 127.0)
IBLOCKS = [(0, 512), (512, 512), (1024, 512), (1536, 512), (2048, 512), (2560, 256)]


class Sem:
    def __init__(self, h):
        self.h, self.n = h, 0

    def inc(self, k=1):
        self.n += k
        return (self.h, k)

    def at(self):
        return (self.h, self.n)


class Bank:
    def __init__(self, t):
        self.t, self.ready, self.free = t, (), ()


def build(debug=None, ncores=8):
    nc = bass.Bass("TRN2", target_bir_lowering=False, detect_race_conditions=False)
    steps = []
    cms = []

    def step(eng, fn, waits=(), incs=()):
        steps.append((eng, tuple(w for w in waits if w and w[1] > 0), fn, tuple(incs)))

    def sem(name):
        cm = nc.semaphore(name)
        cms.append(cm)
        return Sem(cm.__enter__())

    def sbuf(name, shape, dtype):
        cm = nc.sbuf_tensor(name, shape, dtype)
        cms.append(cm)
        t = cm.__enter__()
        return t, cm

    x_d = nc.dram_tensor("x", [T, D], F32, kind="ExternalInput").ap()
    wq_d = nc.dram_tensor("wq", [D, D], F32, kind="ExternalInput").ap()
    wk_d = nc.dram_tensor("wk", [D, D], F32, kind="ExternalInput").ap()
    wv_d = nc.dram_tensor("wv", [D, D], F32, kind="ExternalInput").ap()
    wo_d = nc.dram_tensor("wo", [D, D], F32, kind="ExternalInput").ap()
    rf_d = nc.dram_tensor("rf", [M, DH], F32, kind="ExternalInput").ap()
    clip_x_d = nc.dram_tensor("clip_x", [1], F32, kind="ExternalInput").ap()
    clip_dn_d = nc.dram_tensor("clip_down", [1], F32, kind="ExternalInput").ap()
    wg_d = nc.dram_tensor("w_gate", [FI, D], F32, kind="ExternalInput").ap()
    wu_d = nc.dram_tensor("w_up", [FI, D], F32, kind="ExternalInput").ap()
    wd_d = nc.dram_tensor("w_down", [D, FI], F32, kind="ExternalInput").ap()
    y_d = nc.dram_tensor("y", [T, D], F32, kind="ExternalOutput").ap()

    xfm_d = nc.dram_tensor("xfm", [D, T], F32, kind="Internal").ap()
    y1_d = nc.dram_tensor("y1s", [T, D], F32, kind="Internal").ap()
    ixq_d = nc.dram_tensor("ixq", [T, D], BF16, kind="Internal").ap()
    h_dd = nc.dram_tensor("hs", [FI, T], F32, kind="Internal").ap()
    wgq_d = nc.dram_tensor("wgq", [FI, D], BF16, kind="Internal").ap()
    wuq_d = nc.dram_tensor("wuq", [FI, D], BF16, kind="Internal").ap()
    wdq_d = nc.dram_tensor("wdq", [D, FI], BF16, kind="Internal").ap()
    cc1_in = nc.dram_tensor("cc1i", [1, 128], F32, kind="Internal").ap()
    cc1_out = nc.dram_tensor("cc1o", [1, 128], F32, kind="Internal").ap()
    cc2_in = nc.dram_tensor("cc2i", [1, 128], F32, kind="Internal").ap()
    cc2_out = nc.dram_tensor("cc2o", [1, 128], F32, kind="Internal").ap()

    dbg_outs = []

    def dbg(name, shape, dtype=F32):
        ap = nc.dram_tensor(name, shape, dtype, kind="ExternalOutput").ap()
        dbg_outs.append(name)
        return ap

    ld = sem("ld")
    xb = sem("xb")
    wr = sem("wr")
    te = sem("te")
    ve = sem("ve")
    ac = sem("ac")
    gp = sem("gp")
    cc = sem("cc")

    eye, _ = sbuf("eye", [128, 128], F32)
    ones_mat, _ = sbuf("ones_mat", [128, 128], F32)
    ind2, _ = sbuf("ind2", [128, 2], F32)
    consts, _ = sbuf("consts", [128, 4], F32)
    smalls, _ = sbuf("smalls", [128, 16], F32)
    amax_run, _ = sbuf("amax_run", [128, 1], F32)
    sc128, _ = sbuf("sc128", [1, 128], F32)
    lnst, _ = sbuf("lnst", [128, 16], F32)
    stage, _ = sbuf("stage", [128, 1024], F32)

    rfT, cm_rfT = sbuf("rfT", [128, M], F32)
    wT_a, cm_wTa = sbuf("wT_a", [128, FCH * D], F32)
    wT_b, cm_wTb = sbuf("wT_b", [128, FCH * D], F32)
    xfm_t, cm_xfmt = sbuf("xfm_t", [128, FCH * TTILE], F32)
    xtm_t, cm_xtmt = sbuf("xtm_t", [128, NSUB * D], F32)
    act_fm, cm_actfm = sbuf("act_fm", [128, DCH * TTILE], F32)
    vtm, cm_vtm = sbuf("vtm", [128, NSUB * D], F32)
    proj, cm_proj = sbuf("proj", [128, 1024], F32)
    u_i32, cm_ui = sbuf("u_i32", [128, 1024], I32)
    t_f, cm_tf = sbuf("t_f", [128, 1024], F32)
    r_f, cm_rf = sbuf("r_f", [128, 1024], F32)
    kf, cm_kf = sbuf("kf", [128, 2048], F32)
    ksq, cm_ksq = sbuf("ksq", [128, 512], F32)
    norms_s, cm_ns = sbuf("norms_s", [128, NSUB * HN], F32)
    vs, cm_vs = sbuf("vs", [128, NSUB * 65], F32)
    kv, cm_kv = sbuf("kv", [128, HN * 4 * 65], F32)
    qkvs, cm_qkvs = sbuf("qkvs", [128, NSUB * HN * 65], F32)
    zf, cm_zf = sbuf("zf", [128, 2 * HN], F32)

    banks = []
    for i in range(8):
        cm = nc.psum_tensor(f"pb{i}", [128, 512], F32)
        cms.append(cm)
        banks.append(Bank(cm.__enter__()))

    def te_group(bank, fn, waits=()):
        step("tensor", fn, waits=tuple(bank.free) + tuple(waits), incs=[te.inc()])
        bank.ready = (te.at(),)

    def consume(eng, bank, fn, waits=()):
        s = {"vector": ve, "scalar": ac, "gpsimd": gp}[eng]
        step(eng, fn, waits=tuple(bank.ready) + tuple(waits), incs=[s.inc()])
        bank.free = (s.at(),)

    def chain(eng, fns, waits=(), final_incs=()):
        s = {"vector": ve, "scalar": ac, "gpsimd": gp, "tensor": te}[eng]
        for i, fn in enumerate(fns):
            w = tuple(waits) if i == 0 else (s.at(),)
            incs = [s.inc()] + (list(final_incs) if i == len(fns) - 1 else [])
            step(eng, fn, waits=w, incs=incs)

    step("vector", lambda e: e.memset(eye[:], 1.0))
    step("vector", lambda e: e.memset(ones_mat[:], 1.0))
    step("vector", lambda e: e.memset(ind2[:], 0.0))
    step("vector", lambda e: e.memset(ind2[0:64, 0:1], 1.0))
    step("vector", lambda e: e.memset(ind2[64:128, 1:2], 1.0))
    step("vector", lambda e: e.memset(consts[:, 0:1], LNEPS))
    step("vector", lambda e: e.memset(consts[:, 1:2], NLN16))
    step("vector", lambda e: e.memset(amax_run[:], 0.0))
    step("vector", lambda e: e.memset(kv[:], 0.0), incs=[ve.inc()])
    ve_init = ve.at()
    step("gpsimd", lambda e: e.affine_select(
        eye[:], eye[:], [[1, 128]], ALU.is_equal, 0.0,
        base=0, channel_multiplier=-1), waits=[ve_init], incs=[gp.inc()])
    gp_init = gp.at()

    step("sync", lambda e: e.dma_start(out=smalls[0:1, 8:9], in_=clip_x_d[None, :]),
         incs=[ld.inc(16)])
    step("sync", lambda e: e.dma_start(out=smalls[0:1, 9:10], in_=clip_dn_d[None, :]),
         incs=[ld.inc(16)])
    ld_clip = ld.at()
    step("scalar", lambda e: (
        e.activation(smalls[0:1, 0:1], smalls[0:1, 8:9], AF.Sigmoid),
        e.activation(smalls[0:1, 1:2], smalls[0:1, 9:10], AF.Sigmoid))[-1],
        waits=[ld_clip], incs=[ac.inc()])

    step("sync", lambda e: e.dma_start(out=stage[0:128, 0:DH], in_=rf_d[0:128, :]),
         incs=[ld.inc(16)])
    step("sync", lambda e: e.dma_start(out=stage[0:128, DH:2 * DH], in_=rf_d[128:256, :]),
         incs=[ld.inc(16)])
    ld_rf = ld.at()
    b = banks[0]
    te_group(b, lambda e: (
        e.matmul(b.t[0:64, 0:128], stage[0:128, 0:DH], eye[:],
                 is_transpose=True, start=True, stop=True),
        e.matmul(b.t[0:64, 128:256], stage[0:128, DH:2 * DH], eye[:],
                 is_transpose=True, start=True, stop=True))[-1],
        waits=[ld_rf, gp_init])
    consume("vector", b, lambda e: (
        e.tensor_copy(rfT[0:64, :], b.t[0:64, 0:256]),
        e.tensor_copy(rfT[64:128, :], b.t[0:64, 0:256]))[-1])
    ve_rfT = ve.at()

    for c in range(T // 128):
        step("sync", lambda e, c=c: e.dma_start(
            out=xtm_t[:, 0:D], in_=x_d[c * 128:(c + 1) * 128, :]),
            waits=[te.at()], incs=[ld.inc(16)])
        ld_x = ld.at()
        ac_pre = ac.at()
        for fc in range(FCH):
            bb = banks[fc % 2]
            te_group(bb, (lambda e, fc=fc, bb=bb: e.matmul(
                bb.t[:, 0:128], xtm_t[:, fc * 128:(fc + 1) * 128], eye[:],
                is_transpose=True, start=True, stop=True)), waits=[ld_x, gp_init])
            consume("scalar", bb, (lambda e, fc=fc, bb=bb: e.copy(
                xfm_t[:, fc * 128:(fc + 1) * 128], bb.t[:, 0:128])))
        ac_tr = ac.at()
        dst = xfm_d.rearrange("(fc p) t -> p fc t", p=128)[:, :, c * 128:(c + 1) * 128]
        src = xfm_t[:, 0:1024].rearrange("p (fc t) -> p fc t", t=128)
        step("gpsimd", lambda e, dst=dst, src=src: e.dma_start(out=dst, in_=src),
             waits=[ac_tr], incs=[wr.inc(16)])
        wr_now = wr.at()
        step("scalar", lambda e: e.nop(), waits=[wr_now])
    wr_xfm = wr.at()

    def weight_transpose(w_ap, dest):
        for rc in range(8):
            step("sync", lambda e, rc=rc, w_ap=w_ap: e.dma_start(
                out=xtm_t[:, 0:D], in_=w_ap[rc * 128:(rc + 1) * 128, :]),
                waits=[te.at()], incs=[ld.inc(16)])
            ld_w = ld.at()
            for fc in range(FCH):
                bb = banks[2 + (fc % 2)]
                te_group(bb, (lambda e, fc=fc, bb=bb: e.matmul(
                    bb.t[:, 0:128], xtm_t[:, fc * 128:(fc + 1) * 128], eye[:],
                    is_transpose=True, start=True, stop=True)), waits=[ld_w, gp_init])
                consume("scalar", bb, (lambda e, fc=fc, rc=rc, bb=bb, dest=dest: e.copy(
                    dest[:, fc * D + rc * 128: fc * D + (rc + 1) * 128], bb.t[:, 0:128])))

    weight_transpose(wk_d, wT_a)
    weight_transpose(wv_d, wT_b)
    ac_w = ac.at()

    if debug == "s0":
        d_xfm = dbg("d_xfm", [D, T])
        d_wkT = dbg("d_wkT", [128, FCH * D])
        d_wvT = dbg("d_wvT", [128, FCH * D])
        d_rfT = dbg("d_rfT", [128, M])
        step("gpsimd", lambda e: e.dma_start(out=d_xfm, in_=xfm_d),
             waits=[wr_xfm], incs=[wr.inc(16)])
        step("gpsimd", lambda e: e.dma_start(out=d_wkT, in_=wT_a[:]),
             waits=[ac_w], incs=[wr.inc(16)])
        step("gpsimd", lambda e: e.dma_start(out=d_wvT, in_=wT_b[:]),
             waits=[ac_w], incs=[wr.inc(16)])
        step("gpsimd", lambda e: e.dma_start(out=d_rfT, in_=rfT[:]),
             waits=[ve_rfT], incs=[wr.inc(16)])
        step("gpsimd", lambda e: e.nop(), waits=[wr.at()])
        emit(nc, steps)
        return nc, dbg_outs

    def ln_emit(zsl, dst, ve_z_edge, track_amax):
        chain("vector", [
            (lambda e: e.tensor_reduce(lnst[:, 0:1], zsl, AX.X, ALU.add)),
            (lambda e: e.tensor_scalar(lnst[:, 0:1], lnst[:, 0:1],
                                       1.0 / D, None, ALU.mult)),
            (lambda e: e.tensor_tensor(lnst[:, 2:3], lnst[:, 0:1],
                                       lnst[:, 0:1], ALU.mult)),
        ], waits=[ve_z_edge])
        ve_stats = ve.at()
        step("scalar", (lambda e: e.activation(
            dst, zsl, AF.Square, accum_out=lnst[:, 1:2])),
            waits=[ve_z_edge], incs=[ac.inc()])
        ac_ssq = ac.at()
        chain("vector", [
            (lambda e: e.tensor_scalar(lnst[:, 1:2], lnst[:, 1:2],
                                       1.0 / D, None, ALU.mult)),
            (lambda e: e.tensor_tensor(lnst[:, 1:2], lnst[:, 1:2],
                                       lnst[:, 2:3], ALU.subtract)),
        ], waits=[ac_ssq, ve_stats])
        ve_var = ve.at()
        chain("scalar", [
            (lambda e: e.activation(lnst[:, 3:4], lnst[:, 1:2], AF.Ln,
                                    bias=consts[:, 0:1])),
            (lambda e: e.activation(lnst[:, 4:5], lnst[:, 3:4], AF.Exp,
                                    scale=-0.5)),
        ], waits=[ve_var])
        ac_rstd = ac.at()
        step("vector", (lambda e: e.tensor_scalar(
            dst, zsl, lnst[:, 0:1], None, ALU.subtract)),
            waits=[ve_stats, ac_ssq], incs=[ve.inc()])
        step("vector", (lambda e: e.tensor_scalar(
            dst, dst, lnst[:, 4:5], None, ALU.mult)),
            waits=[ac_rstd], incs=[ve.inc()])
        if track_amax:
            chain("vector", [
                (lambda e: e.tensor_reduce(lnst[:, 6:7], dst, AX.X, ALU.max,
                                           apply_absolute_value=True)),
                (lambda e: e.tensor_tensor(amax_run[:], amax_run[:],
                                           lnst[:, 6:7], ALU.max)),
            ], waits=[ve.at()])
        return ve.at()

    def qk_fm_tile(it, wT, ac_w_edge):
        t0 = it * TTILE
        src = xfm_d.rearrange("(fc p) t -> p fc t", p=128)[:, :, t0:t0 + TTILE]
        dstt = xfm_t.rearrange("p (fc t) -> p fc t", t=TTILE)
        step("sync", lambda e, src=src, dstt=dstt: e.dma_start(out=dstt, in_=src),
             waits=[wr_xfm, te.at()], incs=[ld.inc(16)])
        ld_xfm = ld.at()
        te_prev = te.at()
        for dc in range(DCH):
            bb = banks[dc % 2]
            def kmm(e, dc=dc, bb=bb):
                for fc in range(FCH):
                    inst = e.matmul(
                        bb.t[:],
                        wT[:, fc * D + dc * 128: fc * D + (dc + 1) * 128],
                        xfm_t[:, fc * TTILE:(fc + 1) * TTILE],
                        start=(fc == 0), stop=(fc == FCH - 1))
                return inst
            te_group(bb, kmm, waits=[ld_xfm, ac_w_edge])
            consume("scalar", bb, (lambda e, dc=dc, bb=bb: e.copy(
                act_fm[:, dc * TTILE:(dc + 1) * TTILE], bb.t[:])), waits=[te_prev])
        return ac.at()

    def norms_tile(ac_fm_edge):
        nb = banks[4]
        for dc in range(DCH):
            step("scalar", (lambda e, dc=dc: e.activation(
                ksq[:], act_fm[:, dc * TTILE:(dc + 1) * TTILE], AF.Square)),
                waits=[ac_fm_edge, te.at()] + (list(nb.free) if dc == 0 else []),
                incs=[ac.inc()])
            ac_sq = ac.at()
            def nmm(e, dc=dc, nb=nb):
                for ts2 in range(NSUB):
                    inst = e.matmul(
                        nb.t[:, ts2 * HN + 2 * dc: ts2 * HN + 2 * dc + 2],
                        ksq[:, ts2 * 128:(ts2 + 1) * 128], ind2[:],
                        start=(dc == 0 and ts2 == 0),
                        stop=(dc == DCH - 1 and ts2 == NSUB - 1))
                return inst
            step("tensor", nmm, waits=[ac_sq, gp_init], incs=[te.inc()])
        nb.ready = (te.at(),)
        consume("scalar", nb, lambda e: e.activation(
            norms_s[:, 0:NSUB * HN], nb.t[:, 0:NSUB * HN], AF.Exp,
            scale=-0.5, bias=consts[:, 1:2]))
        return ac.at()

    for it in range(1 if debug == 'a1' else NT):
        ac_kfm = qk_fm_tile(it, wT_a, ac_w)
        ve_prev = ve.at()

        for ts in range(NSUB):
            for dhf in range(2):
                bb = banks[2 + ((ts * 2 + dhf) % 2)]
                def vmm(e, ts=ts, dhf=dhf, bb=bb):
                    for fc in range(FCH):
                        inst = e.matmul(
                            bb.t[:],
                            xfm_t[:, fc * TTILE + ts * 128: fc * TTILE + (ts + 1) * 128],
                            wT_b[:, fc * D + dhf * 512: fc * D + (dhf + 1) * 512],
                            start=(fc == 0), stop=(fc == FCH - 1))
                    return inst
                te_group(bb, vmm, waits=[ld.at(), ac_w])
                consume("vector", bb, (lambda e, ts=ts, dhf=dhf, bb=bb: e.tensor_copy(
                    vtm[:, ts * D + dhf * 512: ts * D + (dhf + 1) * 512], bb.t[:])),
                    waits=[ve_prev])
        ve_vtm = ve.at()

        ac_st = norms_tile(ac_kfm)

        for h in range(HN):
            dc, j = h // 2, h % 2
            pb0, pb1 = banks[5], banks[7]
            def pmm(e, dc=dc, j=j, pb=None, base=0):
                for k in range(2):
                    ts2 = base + k
                    inst = e.matmul(
                        pb.t[:, k * 256:(k + 1) * 256],
                        act_fm[j * 64:(j + 1) * 64,
                               dc * TTILE + ts2 * 128: dc * TTILE + (ts2 + 1) * 128],
                        rfT[j * 64:(j + 1) * 64, :],
                        start=(k == 0), stop=(k == 1))
                return inst
            te_group(pb0, (lambda e, dc=dc, j=j, pb0=pb0: pmm(e, dc, j, pb0, 0)),
                     waits=[ac_kfm, ve_rfT])
            te_group(pb1, (lambda e, dc=dc, j=j, pb1=pb1: pmm(e, dc, j, pb1, 2)),
                     waits=[ac_kfm, ve_rfT])
            consume("scalar", pb0, (lambda e, pb0=pb0: e.copy(proj[:, 0:512], pb0.t[:])),
                    waits=[ve.at()])
            consume("scalar", pb1, (lambda e, pb1=pb1: e.copy(proj[:, 512:1024], pb1.t[:])),
                    waits=[ve.at()])
            ac_proj = ac.at()

            def trig(e):
                e.tensor_scalar(u_i32[:], proj[:], INV2PI, 256.0, ALU.mult, ALU.add)
                e.tensor_scalar(u_i32[:], u_i32[:], -256.0, None, ALU.add)
                e.scalar_tensor_tensor(r_f[:], u_i32[:], NEG2PI, proj[:],
                                       ALU.mult, ALU.add)
                e.tensor_scalar(t_f[:], r_f[:], PI_HI, None, ALU.is_gt)
                e.scalar_tensor_tensor(proj[:], t_f[:], NEG2PI, r_f[:],
                                       ALU.mult, ALU.add)
                e.tensor_scalar(r_f[:], proj[:], PI_LO, -PI_LO, ALU.min, ALU.max)
                e.tensor_scalar(t_f[:], r_f[:], PIHALF, -0.25, ALU.is_gt, ALU.add)
                e.scalar_tensor_tensor(proj[:], t_f[:], NEG2PI, r_f[:],
                                       ALU.mult, ALU.add)
                return e.tensor_scalar(proj[:], proj[:], PI_LO, -PI_LO,
                                       ALU.min, ALU.max)
            step("vector", trig, waits=[ac_proj], incs=[ve.inc()])
            ve_trig = ve.at()

            kf4 = kf.rearrange("p (ts m) -> p ts m", m=MM)
            r4 = r_f.rearrange("p (ts m) -> p ts m", m=M)
            c4 = proj.rearrange("p (ts m) -> p ts m", m=M)
            def sins(e, kf4=kf4, r4=r4, c4=c4):
                e.activation(kf4[:, :, M:MM], r4, AF.Sin)
                return e.activation(kf4[:, :, 0:M], c4, AF.Sin)
            step("scalar", sins, waits=[ve_trig, te.at()], incs=[ac.inc()])
            ac_kf = ac.at()

            def vsfn(e, h=h):
                for ts2 in range(NSUB):
                    sap = norms_s[:, ts2 * HN + h: ts2 * HN + h + 1]
                    e.tensor_scalar(vs[:, ts2 * 65: ts2 * 65 + 64],
                                    vtm[:, ts2 * D + h * 64: ts2 * D + (h + 1) * 64],
                                    sap, None, ALU.mult)
                    inst = e.tensor_copy(vs[:, ts2 * 65 + 64: ts2 * 65 + 65], sap)
                return inst
            step("vector", vsfn, waits=[ve_vtm, ac_st, te.at()], incs=[ve.inc()])
            ve_vs = ve.at()

            kb = banks[6]
            def kvmm(e, kb=kb):
                first = True
                for ts2 in range(NSUB):
                    for ch in range(4):
                        inst = e.matmul(
                            kb.t[:, ch * 65:(ch + 1) * 65],
                            kf[:, ts2 * MM + ch * 128: ts2 * MM + (ch + 1) * 128],
                            vs[:, ts2 * 65:(ts2 + 1) * 65],
                            start=first, stop=(ts2 == NSUB - 1 and ch == 3))
                        first = False
                return inst
            te_group(kb, kvmm, waits=[ac_kf, ve_vs])
            consume("vector", kb, (lambda e, h=h, kb=kb: e.tensor_tensor(
                kv[:, h * 260:(h + 1) * 260], kv[:, h * 260:(h + 1) * 260],
                kb.t[:, 0:260], ALU.add)), waits=[ve_init])

    ve_kv = ve.at()

    if debug in ("a", "a1"):
        d_kv = dbg("d_kv", [128, HN * 4 * 65])
        d_ns = dbg("d_ns", [128, NSUB * HN])
        d_kf = dbg("d_kf", [128, 2048])
        step("gpsimd", lambda e: e.dma_start(out=d_kv, in_=kv[:]),
             waits=[ve_kv], incs=[wr.inc(16)])
        step("gpsimd", lambda e: e.dma_start(out=d_ns, in_=norms_s[:]),
             waits=[ve_kv], incs=[wr.inc(16)])
        step("gpsimd", lambda e: e.dma_start(out=d_kf, in_=kf[:]),
             waits=[ve_kv, ac.at()], incs=[wr.inc(16)])
        step("gpsimd", lambda e: e.nop(), waits=[wr.at()])
        emit(nc, steps)
        return nc, dbg_outs

    weight_transpose(wq_d, wT_a)
    weight_transpose(wo_d, wT_b)
    ac_w2 = ac.at()

    for it in range(NT):
        t0 = it * TTILE
        ac_qfm = qk_fm_tile(it, wT_a, ac_w2)
        ve_prev = ve.at()

        step("sync", lambda e, it=it: e.dma_start(
            out=xtm_t.rearrange("p (ts d) -> p ts d", d=D),
            in_=x_d.rearrange("(nt ts p) d -> nt p ts d", ts=4, p=128)[it]),
            waits=[ve_prev], incs=[ld.inc(16)])
        ld_xtm = ld.at()

        ac_sq_t = norms_tile(ac_qfm)

        for h in range(HN):
            dc, j = h // 2, h % 2
            pb0, pb1 = banks[5], banks[7]
            def pmm0(e, dc=dc, j=j, pb0=pb0):
                return e.matmul(pb0.t[:], rfT[j * 64:(j + 1) * 64, 0:128],
                                act_fm[j * 64:(j + 1) * 64, dc * TTILE:(dc + 1) * TTILE],
                                start=True, stop=True)
            def pmm1(e, dc=dc, j=j, pb1=pb1):
                return e.matmul(pb1.t[:], rfT[j * 64:(j + 1) * 64, 128:256],
                                act_fm[j * 64:(j + 1) * 64, dc * TTILE:(dc + 1) * TTILE],
                                start=True, stop=True)
            te_group(pb0, pmm0, waits=[ac_qfm, ve_rfT])
            te_group(pb1, pmm1, waits=[ac_qfm, ve_rfT])
            consume("scalar", pb0, (lambda e, pb0=pb0: e.copy(proj[:, 0:512], pb0.t[:])),
                    waits=[ve.at()])
            consume("scalar", pb1, (lambda e, pb1=pb1: e.copy(proj[:, 512:1024], pb1.t[:])),
                    waits=[ve.at()])
            ac_proj = ac.at()

            def trig(e):
                e.tensor_scalar(u_i32[:], proj[:], INV2PI, 256.0, ALU.mult, ALU.add)
                e.tensor_scalar(u_i32[:], u_i32[:], -256.0, None, ALU.add)
                e.scalar_tensor_tensor(r_f[:], u_i32[:], NEG2PI, proj[:],
                                       ALU.mult, ALU.add)
                e.tensor_scalar(t_f[:], r_f[:], PI_HI, None, ALU.is_gt)
                e.scalar_tensor_tensor(proj[:], t_f[:], NEG2PI, r_f[:],
                                       ALU.mult, ALU.add)
                e.tensor_scalar(r_f[:], proj[:], PI_LO, -PI_LO, ALU.min, ALU.max)
                e.tensor_scalar(t_f[:], r_f[:], PIHALF, -0.25, ALU.is_gt, ALU.add)
                e.scalar_tensor_tensor(proj[:], t_f[:], NEG2PI, r_f[:],
                                       ALU.mult, ALU.add)
                return e.tensor_scalar(proj[:], proj[:], PI_LO, -PI_LO,
                                       ALU.min, ALU.max)
            step("vector", trig, waits=[ac_proj], incs=[ve.inc()])
            ve_trig = ve.at()
            def sins(e):
                e.activation(kf[:, 2 * 512:4 * 512], r_f[:], AF.Sin)
                return e.activation(kf[:, 0:2 * 512], proj[:], AF.Sin)
            step("scalar", sins, waits=[ve_trig, te.at()], incs=[ac.inc()])
            ac_qf = ac.at()

            for ts2 in range(NSUB):
                qb = banks[6]
                def qkvmm(e, h=h, ts2=ts2, qb=qb):
                    for ch in range(4):
                        inst = e.matmul(
                            qb.t[:, 0:65],
                            kf[:, ch * TTILE + ts2 * 128: ch * TTILE + (ts2 + 1) * 128],
                            kv[:, h * 260 + ch * 65: h * 260 + (ch + 1) * 65],
                            start=(ch == 0), stop=(ch == 3))
                    return inst
                te_group(qb, qkvmm, waits=[ac_qf, ve_kv])
                consume("scalar", qb, (lambda e, h=h, ts2=ts2, qb=qb: e.copy(
                    qkvs[:, (ts2 * HN + h) * 65:(ts2 * HN + h + 1) * 65],
                    qb.t[:, 0:65])), waits=[ve.at()])
        ac_q65_all = ac.at()

        qkr = qkvs.rearrange("p (s c) -> p s c", c=65)
        for ts2 in range(NSUB):
            zcols = qkr[:, ts2 * HN:(ts2 + 1) * HN, 64:65]
            sq_sl = norms_s[:, ts2 * HN:(ts2 + 1) * HN]
            chain("vector", [
                (lambda e, zcols=zcols, sq_sl=sq_sl: e.tensor_tensor(
                    zf[:, 0:HN], zcols, sq_sl, ALU.mult)),
                (lambda e: e.tensor_scalar(zf[:, HN:2 * HN], zf[:, 0:HN],
                                           1e-6, None, ALU.max)),
                (lambda e: e.reciprocal(zf[:, 0:HN], zf[:, HN:2 * HN])),
                (lambda e, sq_sl=sq_sl: e.tensor_tensor(
                    zf[:, HN:2 * HN], zf[:, 0:HN], sq_sl, ALU.mult)),
            ], waits=[ac_q65_all, ac_sq_t])
            ve_zf = ve.at()
            for h in range(HN):
                step("vector", (lambda e, h=h, ts2=ts2: e.tensor_scalar(
                    vtm[:, ts2 * D + h * 64: ts2 * D + (h + 1) * 64],
                    qkvs[:, (ts2 * HN + h) * 65: (ts2 * HN + h) * 65 + 64],
                    zf[:, HN + h: HN + h + 1], None, ALU.mult)),
                    waits=([ve_zf] if h == 0 else ()), incs=[ve.inc()])
        ve_attn = ve.at()

        for ts2 in range(NSUB):
            for dc in range(DCH):
                bb = banks[dc % 2]
                te_group(bb, (lambda e, ts2=ts2, dc=dc, bb=bb: e.matmul(
                    bb.t[:, 0:128], vtm[:, ts2 * D + dc * 128: ts2 * D + (dc + 1) * 128],
                    eye[:], is_transpose=True, start=True, stop=True)),
                    waits=[ve_attn, gp_init])
                consume("scalar", bb, (lambda e, ts2=ts2, dc=dc, bb=bb: e.copy(
                    act_fm[:, dc * TTILE + ts2 * 128: dc * TTILE + (ts2 + 1) * 128],
                    bb.t[:, 0:128])))
        ac_at = ac.at()

        for ts2 in range(NSUB):
            for dhf in range(2):
                bb = banks[2 + dhf]
                def omm(e, ts2=ts2, dhf=dhf, bb=bb):
                    for dc in range(DCH):
                        inst = e.matmul(
                            bb.t[:],
                            act_fm[:, dc * TTILE + ts2 * 128: dc * TTILE + (ts2 + 1) * 128],
                            wT_b[:, dc * D + dhf * 512: dc * D + (dhf + 1) * 512],
                            start=(dc == 0), stop=(dc == DCH - 1))
                    return inst
                te_group(bb, omm, waits=[ac_at, ac_w2])
                consume("vector", bb, (lambda e, ts2=ts2, dhf=dhf, bb=bb: e.tensor_tensor(
                    vtm[:, ts2 * D + dhf * 512: ts2 * D + (dhf + 1) * 512],
                    bb.t[:], xtm_t[:, ts2 * D + dhf * 512: ts2 * D + (dhf + 1) * 512],
                    ALU.add)), waits=[ld_xtm])
            ve_z = ve.at()

            zsl = vtm[:, ts2 * D:(ts2 + 1) * D]
            ve_y1 = ln_emit(zsl, stage[:], ve_z, track_amax=True)
            step("gpsimd", (lambda e, it=it, ts2=ts2: e.dma_start(
                out=y1_d[it * TTILE + ts2 * 128: it * TTILE + (ts2 + 1) * 128, :],
                in_=stage[:])), waits=[ve_y1], incs=[wr.inc(16)])
            wr_y1 = wr.at()
            step("scalar", lambda e: e.nop(), waits=[wr_y1])

    wr_y1_all = wr.at()
    ve_amax = ve.at()

    if debug == "b":
        d_y1 = dbg("d_y1", [T, D])
        d_amax = dbg("d_amax", [128, 1])
        d_ns2 = dbg("d_ns2", [128, NSUB * HN])
        d_kf2 = dbg("d_kf2", [128, 2048])
        d_qkvs = dbg("d_qkvs", [128, NSUB * HN * 65])
        d_attnfm = dbg("d_attnfm", [128, DCH * TTILE])
        d_zf = dbg("d_zf", [128, HN])
        step("gpsimd", lambda e: e.dma_start(out=d_y1, in_=y1_d),
             waits=[wr_y1_all], incs=[wr.inc(16)])
        step("gpsimd", lambda e: e.dma_start(out=d_amax, in_=amax_run[:]),
             waits=[ve_amax], incs=[wr.inc(16)])
        step("gpsimd", lambda e: e.dma_start(out=d_ns2, in_=norms_s[:]),
             waits=[ve_amax], incs=[wr.inc(16)])
        step("gpsimd", lambda e: e.dma_start(out=d_kf2, in_=kf[:]),
             waits=[ve_amax], incs=[wr.inc(16)])
        step("gpsimd", lambda e: e.dma_start(out=d_qkvs, in_=qkvs[:]),
             waits=[ve_amax], incs=[wr.inc(16)])
        step("gpsimd", lambda e: e.dma_start(out=d_attnfm, in_=act_fm[:]),
             waits=[ve_amax], incs=[wr.inc(16)])
        step("gpsimd", lambda e: e.dma_start(out=d_zf, in_=zf[:, 0:HN]),
             waits=[ve_amax], incs=[wr.inc(16)])
        step("gpsimd", lambda e: e.nop(), waits=[wr.at()])
        emit(nc, steps)
        return nc, dbg_outs

    b = banks[0]
    te_group(b, lambda e: e.matmul(b.t[0:1, 0:128], amax_run[:], eye[:],
                                   is_transpose=True, start=True, stop=True),
             waits=[ve_amax, gp_init])
    consume("scalar", b, lambda e: e.copy(sc128[0:1, :], b.t[0:1, 0:128]))
    ac_sc = ac.at()
    step("gpsimd", lambda e: e.dma_start(out=cc1_in, in_=sc128[0:1, :]),
         waits=[ac_sc], incs=[wr.inc(16)])
    step("gpsimd", lambda e: e.collective_compute(
        "AllReduce", ALU.max, replica_groups=[list(range(ncores))],
        ins=[cc1_in], outs=[cc1_out]), waits=[wr.at()], incs=[cc.inc()])
    step("sync", lambda e: e.dma_start(out=sc128[0:1, :], in_=cc1_out),
         waits=[cc.at(), ac_sc], incs=[ld.inc(16)])
    ld_cc1 = ld.at()
    chain("vector", [
        (lambda e: e.tensor_reduce(smalls[0:1, 2:3], sc128[0:1, :], AX.X, ALU.max)),
        (lambda e: e.tensor_tensor(smalls[0:1, 3:4], smalls[0:1, 2:3],
                                   smalls[0:1, 0:1], ALU.mult)),
        (lambda e: e.tensor_scalar(smalls[0:1, 3:4], smalls[0:1, 3:4],
                                   INV127, None, ALU.mult)),
        (lambda e: e.reciprocal(smalls[0:1, 4:5], smalls[0:1, 3:4])),
    ], waits=[ld_cc1, ac.at()])
    ve_sx = ve.at()
    b = banks[1]
    te_group(b, lambda e: e.matmul(b.t[:, 0:2], ones_mat[0:1, 0:128],
                                   smalls[0:1, 3:5], start=True, stop=True),
             waits=[ve_sx, ve_init])
    consume("vector", b, lambda e: e.tensor_copy(smalls[:, 3:5], b.t[:, 0:2]))
    ve_sxb = ve.at()

    bar = [te.at(), ve.at(), ac.at(), gp.at(), ld.at(), wr.at(), xb.at()]
    step("tensor", lambda e: e.nop(), waits=bar)
    step("vector", lambda e: e.nop(), waits=bar)
    step("scalar", lambda e: e.nop(), waits=bar)
    step("gpsimd", lambda e: e.nop(), waits=bar)
    step("sync", lambda e: e.nop(), waits=bar)

    for cm in (cm_zf, cm_qkvs, cm_kv, cm_vs, cm_ns, cm_ksq, cm_kf, cm_rf,
               cm_tf, cm_ui, cm_proj, cm_vtm, cm_actfm, cm_xtmt, cm_xfmt,
               cm_wTb, cm_wTa, cm_rfT):
        cm.__exit__(None, None, None)

    wld, _ = sbuf("wld", [128, FI], F32)
    i8b, _ = sbuf("i8b", [128, FI], I8)
    bfb, _ = sbuf("bfb", [128, FI], BF16)
    y1t, _ = sbuf("y1t", [128, D], F32)
    ixf, _ = sbuf("ixf", [128, HCH * TTILE], BF16)
    silu_t, _ = sbuf("silu_t", [128, TTILE], F32)
    up_t, _ = sbuf("up_t", [128, TTILE], F32)
    h_t, _ = sbuf("h_t", [128, TTILE], F32)
    sgu, _ = sbuf("sgu", [128, 96], F32)
    igT, cm_igT = sbuf("igT", [128, HCH * FI], BF16)
    iupT, cm_iupT = sbuf("iupT", [128, HCH * FI], BF16)

    step("vector", lambda e: e.memset(amax_run[:], 0.0), waits=[ve_sxb], incs=[ve.inc()])
    ve_am0 = ve.at()

    def wquant(w_ap, q_ap, scol, nchunk, width):
        for c_ in range(nchunk):
            step("sync", (lambda e, c_=c_, w_ap=w_ap, width=width: e.dma_start(
                out=wld[:, 0:width], in_=w_ap[c_ * 128:(c_ + 1) * 128, :])),
                waits=[ve.at(), gp.at()], incs=[ld.inc(16)])
            ld_w = ld.at()
            chain("vector", [
                (lambda e, width=width: e.tensor_reduce(
                    lnst[:, 8:9], wld[:, 0:width], AX.X, ALU.max,
                    apply_absolute_value=True)),
                (lambda e, c_=c_, scol=scol: e.tensor_scalar(
                    sgu[:, scol + c_: scol + c_ + 1], lnst[:, 8:9],
                    INV127, None, ALU.mult)),
                (lambda e: e.reciprocal(lnst[:, 9:10], lnst[:, 8:9])),
                (lambda e: e.tensor_scalar(lnst[:, 10:11], lnst[:, 9:10],
                                           127.0, None, ALU.mult)),
                (lambda e, width=width: e.tensor_scalar(
                    i8b[:, 0:width], wld[:, 0:width], lnst[:, 10:11],
                    None, ALU.mult)),
                (lambda e, width=width: e.tensor_copy(
                    bfb[:, 0:width], i8b[:, 0:width])),
            ], waits=[ld_w, wr.at()])
            ve_q = ve.at()
            step("gpsimd", (lambda e, c_=c_, q_ap=q_ap, width=width: e.dma_start(
                out=q_ap[c_ * 128:(c_ + 1) * 128, :], in_=bfb[:, 0:width])),
                waits=[ve_q], incs=[wr.inc(16)])

    wquant(wg_d, wgq_d, 0, ICH, D)
    wquant(wu_d, wuq_d, 22, ICH, D)
    wquant(wd_d, wdq_d, 44, HCH, FI)
    wr_wq = wr.at()

    step("vector", lambda e: (
        e.tensor_scalar(sgu[:, 52:74], sgu[:, 0:22], smalls[:, 3:4], None, ALU.mult),
        e.tensor_scalar(sgu[:, 74:96], sgu[:, 22:44], smalls[:, 3:4], None, ALU.mult))[-1],
        waits=[ve_sxb], incs=[ve.inc()])
    ve_sgx = ve.at()

    for hc in range(HCH):
        for (i0, ilen) in IBLOCKS:
            step("sync", (lambda e, hc=hc, i0=i0, ilen=ilen: e.dma_start(
                out=igT[:, hc * FI + i0: hc * FI + i0 + ilen],
                in_=wgq_d[i0:i0 + ilen, hc * 128:(hc + 1) * 128], transpose=True)),
                waits=[wr_wq], incs=[xb.inc(16)])
            step("sync", (lambda e, hc=hc, i0=i0, ilen=ilen: e.dma_start(
                out=iupT[:, hc * FI + i0: hc * FI + i0 + ilen],
                in_=wuq_d[i0:i0 + ilen, hc * 128:(hc + 1) * 128], transpose=True)),
                waits=[wr_wq], incs=[xb.inc(16)])
    xb_w = xb.at()

    for it in range(NT):
        t0 = it * TTILE
        for ts2 in range(NSUB):
            step("sync", (lambda e, t0=t0, ts2=ts2: e.dma_start(
                out=y1t[:], in_=y1_d[t0 + ts2 * 128: t0 + (ts2 + 1) * 128, :])),
                waits=[wr_y1_all, ve.at()], incs=[ld.inc(16)])
            ld_y1 = ld.at()
            step("vector", (lambda e: (
                e.tensor_scalar(y1t[:], y1t[:], smalls[:, 4:5], 127.0,
                                ALU.mult, ALU.min),
                e.tensor_scalar(i8b[:, 0:D], y1t[:], -128.0, None, ALU.max),
                e.tensor_copy(bfb[:, 0:D], i8b[:, 0:D]))[-1]),
                waits=[ld_y1, ve_sxb, wr.at()], incs=[ve.inc()])
            ve_ix = ve.at()
            step("gpsimd", (lambda e, t0=t0, ts2=ts2: e.dma_start(
                out=ixq_d[t0 + ts2 * 128: t0 + (ts2 + 1) * 128, :], in_=bfb[:, 0:D])),
                waits=[ve_ix], incs=[wr.inc(16)])
        wr_ix = wr.at()
        for hc in range(HCH):
            step("sync", (lambda e, t0=t0, hc=hc: e.dma_start(
                out=ixf[:, hc * TTILE:(hc + 1) * TTILE],
                in_=ixq_d[t0:t0 + TTILE, hc * 128:(hc + 1) * 128], transpose=True)),
                waits=[wr_ix, te.at()], incs=[xb.inc(16)])
        xb_ix = xb.at()

        for ic in range(ICH):
            gb = banks[0 + (ic % 2)]
            ub = banks[2 + (ic % 2)]
            def gmm(e, ic=ic, gb=gb):
                for hc in range(HCH):
                    inst = e.matmul(
                        gb.t[:], igT[:, hc * FI + ic * 128: hc * FI + (ic + 1) * 128],
                        ixf[:, hc * TTILE:(hc + 1) * TTILE],
                        start=(hc == 0), stop=(hc == HCH - 1))
                return inst
            def umm(e, ic=ic, ub=ub):
                for hc in range(HCH):
                    inst = e.matmul(
                        ub.t[:], iupT[:, hc * FI + ic * 128: hc * FI + (ic + 1) * 128],
                        ixf[:, hc * TTILE:(hc + 1) * TTILE],
                        start=(hc == 0), stop=(hc == HCH - 1))
                return inst
            te_group(gb, gmm, waits=[xb_ix, xb_w])
            te_group(ub, umm, waits=[xb_ix, xb_w])
            consume("scalar", gb, (lambda e, ic=ic, gb=gb: e.activation(
                silu_t[:], gb.t[:], AF.Silu, scale=sgu[:, 52 + ic: 53 + ic])),
                waits=[ve_sgx, ve.at()])
            ac_si = ac.at()
            consume("vector", ub, (lambda e, ic=ic, ub=ub: e.tensor_scalar(
                up_t[:], ub.t[:], sgu[:, 74 + ic: 75 + ic], None, ALU.mult)),
                waits=[ve_sgx, wr.at()])
            ve_up = ve.at()
            chain("vector", [
                (lambda e: e.tensor_tensor(h_t[:], silu_t[:], up_t[:], ALU.mult)),
                (lambda e: e.tensor_reduce(lnst[:, 12:13], h_t[:], AX.X, ALU.max,
                                           apply_absolute_value=True)),
                (lambda e: e.tensor_tensor(amax_run[:], amax_run[:],
                                           lnst[:, 12:13], ALU.max)),
            ], waits=[ac_si, ve_up, ve_am0, wr.at()])
            ve_h = ve.at()
            step("gpsimd", (lambda e, t0=t0, ic=ic: e.dma_start(
                out=h_dd[ic * 128:(ic + 1) * 128, t0:t0 + TTILE], in_=h_t[:])),
                waits=[ve_h], incs=[wr.inc(16)])
    ve_amh = ve.at()
    wr_h = wr.at()

    b = banks[4]
    te_group(b, lambda e: e.matmul(b.t[0:1, 0:128], amax_run[:], eye[:],
                                   is_transpose=True, start=True, stop=True),
             waits=[ve_amh, gp_init])
    consume("scalar", b, lambda e: e.copy(sc128[0:1, :], b.t[0:1, 0:128]),
            waits=[ld.at()])
    ac_sc2 = ac.at()
    step("gpsimd", lambda e: e.dma_start(out=cc2_in, in_=sc128[0:1, :]),
         waits=[ac_sc2], incs=[wr.inc(16)])
    step("gpsimd", lambda e: e.collective_compute(
        "AllReduce", ALU.max, replica_groups=[list(range(ncores))],
        ins=[cc2_in], outs=[cc2_out]), waits=[wr.at()], incs=[cc.inc()])
    step("sync", lambda e: e.dma_start(out=sc128[0:1, :], in_=cc2_out),
         waits=[cc.at(), ac_sc2], incs=[ld.inc(16)])
    ld_cc2 = ld.at()
    chain("vector", [
        (lambda e: e.tensor_reduce(smalls[0:1, 5:6], sc128[0:1, :], AX.X, ALU.max)),
        (lambda e: e.tensor_tensor(smalls[0:1, 6:7], smalls[0:1, 5:6],
                                   smalls[0:1, 1:2], ALU.mult)),
        (lambda e: e.tensor_scalar(smalls[0:1, 6:7], smalls[0:1, 6:7],
                                   INV127, None, ALU.mult)),
        (lambda e: e.reciprocal(smalls[0:1, 7:8], smalls[0:1, 6:7])),
    ], waits=[ld_cc2, ac.at()])
    ve_sh = ve.at()
    b = banks[5]
    te_group(b, lambda e: e.matmul(b.t[:, 0:2], ones_mat[0:1, 0:128],
                                   smalls[0:1, 6:8], start=True, stop=True),
             waits=[ve_sh, ve_init])
    consume("vector", b, lambda e: e.tensor_copy(smalls[:, 6:8], b.t[:, 0:2]))
    ve_shb = ve.at()

    cm_iupT.__exit__(None, None, None)
    cm_igT.__exit__(None, None, None)
    idT, _ = sbuf("idT", [128, ICH * D], BF16)
    ihf, _ = sbuf("ihf", [128, ICH * TTILE], BF16)
    hld, _ = sbuf("hld", [128, TTILE], F32)
    sdx, _ = sbuf("sdx", [128, D], F32)
    zb, _ = sbuf("zb", [128, D], F32)

    for ic in range(ICH):
        for hb in range(2):
            step("sync", (lambda e, ic=ic, hb=hb: e.dma_start(
                out=idT[:, ic * D + hb * 512: ic * D + (hb + 1) * 512],
                in_=wdq_d[hb * 512:(hb + 1) * 512, ic * 128:(ic + 1) * 128],
                transpose=True)), waits=[wr_wq, te.at()], incs=[xb.inc(16)])
    xb_d = xb.at()

    sd_d = nc.dram_tensor("sdd", [128, 8], F32, kind="Internal").ap()
    step("gpsimd", lambda e: e.dma_start(out=sd_d, in_=sgu[:, 44:52]),
         waits=[ve.at()], incs=[wr.inc(16)])
    wr_sd = wr.at()
    sd_row = stage[0:1, 0:1024].rearrange("p (hb jj) -> p hb jj", jj=128)
    step("sync", lambda e: e.dma_start(
        out=sd_row, in_=sd_d.rearrange("jj hb -> hb jj")[None],
        allow_slow_non_contiguous=True),
        waits=[wr_sd, ac.at(), ve.at()], incs=[ld.inc(16)])
    ld_sdr = ld.at()
    for nh in range(2):
        b = banks[6 + nh]
        te_group(b, (lambda e, nh=nh, b=b: e.matmul(
            b.t[:], ones_mat[0:1, 0:128], stage[0:1, nh * 512:(nh + 1) * 512],
            start=True, stop=True)), waits=[ld_sdr, ve_init])
        consume("vector", b, (lambda e, nh=nh, b=b: e.tensor_copy(
            sdx[:, nh * 512:(nh + 1) * 512], b.t[:])))
    step("vector", lambda e: e.tensor_scalar(sdx[:], sdx[:], smalls[:, 6:7],
                                             None, ALU.mult),
         waits=[ve_shb], incs=[ve.inc()])
    ve_sdx = ve.at()

    for it in range(NT):
        t0 = it * TTILE
        for ic in range(ICH):
            step("sync", (lambda e, t0=t0, ic=ic: e.dma_start(
                out=hld[:], in_=h_dd[ic * 128:(ic + 1) * 128, t0:t0 + TTILE])),
                waits=[wr_h, ve.at()], incs=[ld.inc(16)])
            ld_h = ld.at()
            step("vector", (lambda e, ic=ic: (
                e.tensor_scalar(hld[:], hld[:], smalls[:, 7:8], 127.0,
                                ALU.mult, ALU.min),
                e.tensor_scalar(i8b[:, 0:TTILE], hld[:], -128.0, None, ALU.max),
                e.tensor_copy(ihf[:, ic * TTILE:(ic + 1) * TTILE], i8b[:, 0:TTILE]))[-1]),
                waits=[ld_h, ve_shb, te.at()], incs=[ve.inc()])
        ve_ih = ve.at()

        for ts2 in range(NSUB):
            for hb in range(2):
                bb = banks[0 + hb]
                def dmm(e, ts2=ts2, hb=hb, bb=bb):
                    for ic in range(ICH):
                        inst = e.matmul(
                            bb.t[:],
                            ihf[:, ic * TTILE + ts2 * 128: ic * TTILE + (ts2 + 1) * 128],
                            idT[:, ic * D + hb * 512: ic * D + (hb + 1) * 512],
                            start=(ic == 0), stop=(ic == ICH - 1))
                    return inst
                te_group(bb, dmm, waits=[ve_ih, xb_d])
            step("sync", (lambda e, t0=t0, ts2=ts2: e.dma_start(
                out=y1t[:], in_=y1_d[t0 + ts2 * 128: t0 + (ts2 + 1) * 128, :])),
                waits=[ve.at()], incs=[ld.inc(16)])
            ld_y1b = ld.at()
            for hb in range(2):
                bb = banks[0 + hb]
                consume("vector", bb, (lambda e, hb=hb, bb=bb: (
                    e.tensor_tensor(zb[:, hb * 512:(hb + 1) * 512], bb.t[:],
                                    sdx[:, hb * 512:(hb + 1) * 512], ALU.mult),
                    e.tensor_tensor(zb[:, hb * 512:(hb + 1) * 512],
                                    zb[:, hb * 512:(hb + 1) * 512],
                                    y1t[:, hb * 512:(hb + 1) * 512], ALU.add))[-1]),
                    waits=[ld_y1b, ve_sdx])
            ve_z2 = ve.at()

            ve_out = ln_emit(zb[:], stage[:], ve_z2, track_amax=False)
            step("gpsimd", (lambda e, t0=t0, ts2=ts2: e.dma_start(
                out=y_d[t0 + ts2 * 128: t0 + (ts2 + 1) * 128, :], in_=stage[:])),
                waits=[ve_out], incs=[wr.inc(16)])
            wr_o = wr.at()
            step("scalar", lambda e: e.nop(), waits=[wr_o])

    step("gpsimd", lambda e: e.nop(), waits=[wr.at()])
    emit(nc, steps)
    return nc, dbg_outs


def emit(nc, steps):
    with nc.Block() as block:
        for eng_name in ("sync", "tensor", "vector", "scalar", "gpsimd"):
            mine = [(w, f, i) for (e, w, f, i) in steps if e == eng_name]

            def run(eng, mine=mine):
                for waits, fn, incs in mine:
                    for (h, v) in waits:
                        eng.wait_ge(h, v)
                    inst = fn(eng)
                    for (h, k) in incs:
                        inst = inst.then_inc(h, k)
            getattr(block, eng_name)(run)


def kernel(L0_attn_x, L0_attn_wq, L0_attn_wk, L0_attn_wv, L0_attn_wo,
           L0_attn_rf, L0_ffn_clip_x, L0_ffn_clip_down, L0_ffn_w_gate,
           L0_ffn_w_up, L0_ffn_w_down):
    nc, _ = build()
    base = {
        "wq": np.ascontiguousarray(L0_attn_wq),
        "wk": np.ascontiguousarray(L0_attn_wk),
        "wv": np.ascontiguousarray(L0_attn_wv),
        "wo": np.ascontiguousarray(L0_attn_wo),
        "rf": np.ascontiguousarray(L0_attn_rf),
        "clip_x": np.ascontiguousarray(L0_ffn_clip_x),
        "clip_down": np.ascontiguousarray(L0_ffn_clip_down),
        "w_gate": np.ascontiguousarray(L0_ffn_w_gate),
        "w_up": np.ascontiguousarray(L0_ffn_w_up),
        "w_down": np.ascontiguousarray(L0_ffn_w_down),
    }
    in_maps = [dict(base, x=np.ascontiguousarray(L0_attn_x[c])) for c in range(8)]
    res = run_bass_kernel_spmd(nc, in_maps, core_ids=list(range(8)))
    return np.stack([res.results[c]["y"] for c in range(8)], axis=0)


# version 02 (final)
# speedup vs baseline: 1.0438x; 1.0438x over previous
import math
import numpy as np
import ml_dtypes
import concourse.bass as bass
import concourse.mybir as mybir
from concourse.bass_utils import run_bass_kernel_spmd

F32 = mybir.dt.float32
BF16 = mybir.dt.bfloat16
I8 = mybir.dt.int8
I32 = mybir.dt.int32
AF = mybir.ActivationFunctionType
ALU = mybir.AluOpType
AX = mybir.AxisListType

T, D, HN, DH, M, MM, FI = 4096, 1024, 16, 64, 256, 512, 2816
TTILE, NT, NSUB = 512, 8, 4
DCH, FCH, ICH, HCH = 8, 8, 22, 8
INV2PI = float(1.0 / (2 * math.pi))
NEG2PI = float(-2 * math.pi)
PIHALF = float(math.pi / 2)
PI_HI = float(np.float32(math.pi))
PI_LO = 3.1415925
NLN16 = float(-math.log(16.0))
LNEPS = 1e-5
INV127 = float(1.0 / 127.0)
IBLOCKS = [(0, 512), (512, 512), (1024, 512), (1536, 512), (2048, 512), (2560, 256)]


class Sem:
    def __init__(self, h):
        self.h, self.n = h, 0

    def inc(self, k=1):
        self.n += k
        return (self.h, k)

    def at(self):
        return (self.h, self.n)


class Bank:
    def __init__(self, t):
        self.t, self.ready, self.free = t, (), ()


def build(debug=None, ncores=8):
    nc = bass.Bass("TRN2", target_bir_lowering=False, detect_race_conditions=False)
    steps = []
    cms = []

    def step(eng, fn, waits=(), incs=()):
        steps.append((eng, tuple(w for w in waits if w and w[1] > 0), fn, tuple(incs)))

    def sem(name):
        cm = nc.semaphore(name)
        cms.append(cm)
        return Sem(cm.__enter__())

    def sbuf(name, shape, dtype):
        cm = nc.sbuf_tensor(name, shape, dtype)
        cms.append(cm)
        t = cm.__enter__()
        return t, cm

    x_d = nc.dram_tensor("x", [T, D], F32, kind="ExternalInput").ap()
    wq_d = nc.dram_tensor("wq", [D, D], F32, kind="ExternalInput").ap()
    wk_d = nc.dram_tensor("wk", [D, D], F32, kind="ExternalInput").ap()
    wv_d = nc.dram_tensor("wv", [D, D], F32, kind="ExternalInput").ap()
    wo_d = nc.dram_tensor("wo", [D, D], F32, kind="ExternalInput").ap()
    rf_d = nc.dram_tensor("rf", [M, DH], F32, kind="ExternalInput").ap()
    clip_x_d = nc.dram_tensor("clip_x", [1], F32, kind="ExternalInput").ap()
    clip_dn_d = nc.dram_tensor("clip_down", [1], F32, kind="ExternalInput").ap()
    wg_d = nc.dram_tensor("w_gate", [FI, D], F32, kind="ExternalInput").ap()
    wu_d = nc.dram_tensor("w_up", [FI, D], F32, kind="ExternalInput").ap()
    wd_d = nc.dram_tensor("w_down", [D, FI], F32, kind="ExternalInput").ap()
    y_d = nc.dram_tensor("y", [T, D], F32, kind="ExternalOutput").ap()

    xfm_d = nc.dram_tensor("xfm", [D, T], F32, kind="Internal").ap()
    y1_d = nc.dram_tensor("y1s", [T, D], F32, kind="Internal").ap()
    ixq_d = nc.dram_tensor("ixq", [T, D], BF16, kind="Internal").ap()
    h_dd = nc.dram_tensor("hs", [FI, T], F32, kind="Internal").ap()
    wgq_d = nc.dram_tensor("wgq", [FI, D], BF16, kind="Internal").ap()
    wuq_d = nc.dram_tensor("wuq", [FI, D], BF16, kind="Internal").ap()
    wdq_d = nc.dram_tensor("wdq", [D, FI], BF16, kind="Internal").ap()
    cc1_in = nc.dram_tensor("cc1i", [1, 128], F32, kind="Internal").ap()
    cc1_out = nc.dram_tensor("cc1o", [1, 128], F32, kind="Internal").ap()
    cc2_in = nc.dram_tensor("cc2i", [1, 128], F32, kind="Internal").ap()
    cc2_out = nc.dram_tensor("cc2o", [1, 128], F32, kind="Internal").ap()

    dbg_outs = []

    def dbg(name, shape, dtype=F32):
        ap = nc.dram_tensor(name, shape, dtype, kind="ExternalOutput").ap()
        dbg_outs.append(name)
        return ap

    ld = sem("ld")
    xb = sem("xb")
    wr = sem("wr")
    te = sem("te")
    ve = sem("ve")
    ac = sem("ac")
    gp = sem("gp")
    cc = sem("cc")

    eye, _ = sbuf("eye", [128, 128], F32)
    ones_mat, _ = sbuf("ones_mat", [128, 128], F32)
    ind2, _ = sbuf("ind2", [128, 2], F32)
    consts, _ = sbuf("consts", [128, 4], F32)
    smalls, _ = sbuf("smalls", [128, 16], F32)
    amax_run, _ = sbuf("amax_run", [128, 1], F32)
    lnst, _ = sbuf("lnst", [128, 16], F32)
    stage, _ = sbuf("stage", [128, 1024], F32)
    sc128 = stage

    rfT, cm_rfT = sbuf("rfT", [128, M], F32)
    wT_a, cm_wTa = sbuf("wT_a", [128, FCH * D], F32)
    wT_b, cm_wTb = sbuf("wT_b", [128, FCH * D], F32)
    xfm_t, cm_xfmt = sbuf("xfm_t", [128, FCH * TTILE], F32)
    xtm_t, cm_xtmt = sbuf("xtm_t", [128, 2 * D], F32)
    act_fm, cm_actfm = sbuf("act_fm", [128, DCH * TTILE], F32)
    vtm, cm_vtm = sbuf("vtm", [128, NSUB * D], F32)
    proj2, cm_proj = sbuf("proj2", [128, 2 * 1024], F32)
    u_i32, cm_ui = sbuf("u_i32", [128, 1024], I32)
    t_f, cm_tf = sbuf("t_f", [128, 1024], F32)
    r_f2, cm_rf = sbuf("r_f2", [128, 2 * 1024], F32)
    kf2, cm_kf = sbuf("kf2", [128, 2 * 2048], F32)
    norms_s, cm_ns = sbuf("norms_s", [128, NSUB * HN], F32)
    vs2, cm_vs = sbuf("vs2", [128, 2 * NSUB * 65], F32)
    ksq = stage[:, 0:512]
    kv, cm_kv = sbuf("kv", [128, HN * 4 * 65], F32)
    qkvs, cm_qkvs = sbuf("qkvs", [128, NSUB * HN * 65], F32)
    zf, cm_zf = sbuf("zf", [128, 2 * HN], F32)

    banks = []
    for i in range(8):
        cm = nc.psum_tensor(f"pb{i}", [128, 512], F32)
        cms.append(cm)
        banks.append(Bank(cm.__enter__()))

    def te_group(bank, fn, waits=()):
        step("tensor", fn, waits=tuple(bank.free) + tuple(waits), incs=[te.inc()])
        bank.ready = (te.at(),)

    def consume(eng, bank, fn, waits=()):
        s = {"vector": ve, "scalar": ac, "gpsimd": gp}[eng]
        step(eng, fn, waits=tuple(bank.ready) + tuple(waits), incs=[s.inc()])
        bank.free = (s.at(),)

    def chain(eng, fns, waits=(), final_incs=()):
        s = {"vector": ve, "scalar": ac, "gpsimd": gp, "tensor": te}[eng]
        for i, fn in enumerate(fns):
            w = tuple(waits) if i == 0 else (s.at(),)
            incs = [s.inc()] + (list(final_incs) if i == len(fns) - 1 else [])
            step(eng, fn, waits=w, incs=incs)

    step("vector", lambda e: e.memset(eye[:], 1.0))
    step("vector", lambda e: e.memset(ones_mat[:], 1.0))
    step("vector", lambda e: e.memset(ind2[:], 0.0))
    step("vector", lambda e: e.memset(ind2[0:64, 0:1], 1.0))
    step("vector", lambda e: e.memset(ind2[64:128, 1:2], 1.0))
    step("vector", lambda e: e.memset(consts[:, 0:1], LNEPS))
    step("vector", lambda e: e.memset(consts[:, 1:2], NLN16))
    step("vector", lambda e: e.memset(amax_run[:], 0.0))
    step("vector", lambda e: e.memset(kv[:], 0.0), incs=[ve.inc()])
    ve_init = ve.at()
    step("gpsimd", lambda e: e.affine_select(
        eye[:], eye[:], [[1, 128]], ALU.is_equal, 0.0,
        base=0, channel_multiplier=-1), waits=[ve_init], incs=[gp.inc()])
    gp_init = gp.at()

    step("sync", lambda e: e.dma_start(out=smalls[0:1, 8:9], in_=clip_x_d[None, :]),
         incs=[ld.inc(16)])
    step("sync", lambda e: e.dma_start(out=smalls[0:1, 9:10], in_=clip_dn_d[None, :]),
         incs=[ld.inc(16)])
    ld_clip = ld.at()
    step("scalar", lambda e: (
        e.activation(smalls[0:1, 0:1], smalls[0:1, 8:9], AF.Sigmoid),
        e.activation(smalls[0:1, 1:2], smalls[0:1, 9:10], AF.Sigmoid))[-1],
        waits=[ld_clip], incs=[ac.inc()])

    step("sync", lambda e: e.dma_start(out=stage[0:128, 0:DH], in_=rf_d[0:128, :]),
         incs=[ld.inc(16)])
    step("sync", lambda e: e.dma_start(out=stage[0:128, DH:2 * DH], in_=rf_d[128:256, :]),
         incs=[ld.inc(16)])
    ld_rf = ld.at()
    b = banks[0]
    te_group(b, lambda e: (
        e.matmul(b.t[0:64, 0:128], stage[0:128, 0:DH], eye[:],
                 is_transpose=True, start=True, stop=True),
        e.matmul(b.t[0:64, 128:256], stage[0:128, DH:2 * DH], eye[:],
                 is_transpose=True, start=True, stop=True))[-1],
        waits=[ld_rf, gp_init])
    consume("vector", b, lambda e: (
        e.tensor_copy(rfT[0:64, :], b.t[0:64, 0:256]),
        e.tensor_copy(rfT[64:128, :], b.t[0:64, 0:256]))[-1])
    ve_rfT = ve.at()

    for c in range(T // 128):
        step("sync", lambda e, c=c: e.dma_start(
            out=xtm_t[:, 0:D], in_=x_d[c * 128:(c + 1) * 128, :]),
            waits=[te.at()], incs=[ld.inc(16)])
        ld_x = ld.at()
        ac_pre = ac.at()
        for fc in range(FCH):
            bb = banks[fc % 2]
            te_group(bb, (lambda e, fc=fc, bb=bb: e.matmul(
                bb.t[:, 0:128], xtm_t[:, fc * 128:(fc + 1) * 128], eye[:],
                is_transpose=True, start=True, stop=True)), waits=[ld_x, gp_init])
            consume("scalar", bb, (lambda e, fc=fc, bb=bb: e.copy(
                xfm_t[:, fc * 128:(fc + 1) * 128], bb.t[:, 0:128])))
        ac_tr = ac.at()
        dst = xfm_d.rearrange("(fc p) t -> p fc t", p=128)[:, :, c * 128:(c + 1) * 128]
        src = xfm_t[:, 0:1024].rearrange("p (fc t) -> p fc t", t=128)
        step("gpsimd", lambda e, dst=dst, src=src: e.dma_start(out=dst, in_=src),
             waits=[ac_tr], incs=[wr.inc(16)])
        wr_now = wr.at()
        step("scalar", lambda e: e.nop(), waits=[wr_now])
    wr_xfm = wr.at()

    def weight_transpose(w_ap, dest):
        for rc in range(8):
            step("sync", lambda e, rc=rc, w_ap=w_ap: e.dma_start(
                out=xtm_t[:, 0:D], in_=w_ap[rc * 128:(rc + 1) * 128, :]),
                waits=[te.at()], incs=[ld.inc(16)])
            ld_w = ld.at()
            for fc in range(FCH):
                bb = banks[2 + (fc % 2)]
                te_group(bb, (lambda e, fc=fc, bb=bb: e.matmul(
                    bb.t[:, 0:128], xtm_t[:, fc * 128:(fc + 1) * 128], eye[:],
                    is_transpose=True, start=True, stop=True)), waits=[ld_w, gp_init])
                consume("scalar", bb, (lambda e, fc=fc, rc=rc, bb=bb, dest=dest: e.copy(
                    dest[:, fc * D + rc * 128: fc * D + (rc + 1) * 128], bb.t[:, 0:128])))

    weight_transpose(wk_d, wT_a)
    weight_transpose(wv_d, wT_b)
    ac_w = ac.at()

    if debug == "s0":
        d_xfm = dbg("d_xfm", [D, T])
        d_wkT = dbg("d_wkT", [128, FCH * D])
        d_wvT = dbg("d_wvT", [128, FCH * D])
        d_rfT = dbg("d_rfT", [128, M])
        step("gpsimd", lambda e: e.dma_start(out=d_xfm, in_=xfm_d),
             waits=[wr_xfm], incs=[wr.inc(16)])
        step("gpsimd", lambda e: e.dma_start(out=d_wkT, in_=wT_a[:]),
             waits=[ac_w], incs=[wr.inc(16)])
        step("gpsimd", lambda e: e.dma_start(out=d_wvT, in_=wT_b[:]),
             waits=[ac_w], incs=[wr.inc(16)])
        step("gpsimd", lambda e: e.dma_start(out=d_rfT, in_=rfT[:]),
             waits=[ve_rfT], incs=[wr.inc(16)])
        step("gpsimd", lambda e: e.nop(), waits=[wr.at()])
        emit(nc, steps)
        return nc, dbg_outs

    def ln_emit(zsl, dst, ve_z_edge, track_amax):
        chain("vector", [
            (lambda e: e.tensor_reduce(lnst[:, 0:1], zsl, AX.X, ALU.add)),
            (lambda e: e.tensor_scalar(lnst[:, 0:1], lnst[:, 0:1],
                                       1.0 / D, None, ALU.mult)),
            (lambda e: e.tensor_tensor(lnst[:, 2:3], lnst[:, 0:1],
                                       lnst[:, 0:1], ALU.mult)),
        ], waits=[ve_z_edge])
        ve_stats = ve.at()
        step("scalar", (lambda e: e.activation(
            dst, zsl, AF.Square, accum_out=lnst[:, 1:2])),
            waits=[ve_z_edge], incs=[ac.inc()])
        ac_ssq = ac.at()
        chain("vector", [
            (lambda e: e.tensor_scalar(lnst[:, 1:2], lnst[:, 1:2],
                                       1.0 / D, None, ALU.mult)),
            (lambda e: e.tensor_tensor(lnst[:, 1:2], lnst[:, 1:2],
                                       lnst[:, 2:3], ALU.subtract)),
        ], waits=[ac_ssq, ve_stats])
        ve_var = ve.at()
        chain("scalar", [
            (lambda e: e.activation(lnst[:, 3:4], lnst[:, 1:2], AF.Ln,
                                    bias=consts[:, 0:1])),
            (lambda e: e.activation(lnst[:, 4:5], lnst[:, 3:4], AF.Exp,
                                    scale=-0.5)),
        ], waits=[ve_var])
        ac_rstd = ac.at()
        step("vector", (lambda e: e.tensor_scalar(
            dst, zsl, lnst[:, 0:1], None, ALU.subtract)),
            waits=[ve_stats, ac_ssq], incs=[ve.inc()])
        step("vector", (lambda e: e.tensor_scalar(
            dst, dst, lnst[:, 4:5], None, ALU.mult)),
            waits=[ac_rstd], incs=[ve.inc()])
        if track_amax:
            chain("vector", [
                (lambda e: e.tensor_reduce(lnst[:, 6:7], dst, AX.X, ALU.max,
                                           apply_absolute_value=True)),
                (lambda e: e.tensor_tensor(amax_run[:], amax_run[:],
                                           lnst[:, 6:7], ALU.max)),
            ], waits=[ve.at()])
        return ve.at()

    xfm_rd_mark = [(te.h, 0)]

    def qk_fm_tile(it, wT, ac_w_edge, act_fm_rd):
        t0 = it * TTILE
        src = xfm_d.rearrange("(fc p) t -> p fc t", p=128)[:, :, t0:t0 + TTILE]
        dstt = xfm_t.rearrange("p (fc t) -> p fc t", t=TTILE)
        step("sync", lambda e, src=src, dstt=dstt: e.dma_start(out=dstt, in_=src),
             waits=[wr_xfm, xfm_rd_mark[0]], incs=[ld.inc(16)])
        ld_xfm = ld.at()
        for dc in range(DCH):
            bb = banks[dc % 2]
            def kmm(e, dc=dc, bb=bb):
                for fc in range(FCH):
                    inst = e.matmul(
                        bb.t[:],
                        wT[:, fc * D + dc * 128: fc * D + (dc + 1) * 128],
                        xfm_t[:, fc * TTILE:(fc + 1) * TTILE],
                        start=(fc == 0), stop=(fc == FCH - 1))
                return inst
            te_group(bb, kmm, waits=[ld_xfm, ac_w_edge])
            consume("scalar", bb, (lambda e, dc=dc, bb=bb: e.copy(
                act_fm[:, dc * TTILE:(dc + 1) * TTILE], bb.t[:])), waits=[act_fm_rd])
        return ac.at()

    def norms_tile(ac_fm_edge, norms_rd):
        nb = banks[4]
        for dc in range(DCH):
            step("scalar", (lambda e, dc=dc: e.activation(
                ksq[:], act_fm[:, dc * TTILE:(dc + 1) * TTILE], AF.Square)),
                waits=[ac_fm_edge, te.at()] + (list(nb.free) if dc == 0 else []),
                incs=[ac.inc()])
            ac_sq = ac.at()
            def nmm(e, dc=dc, nb=nb):
                for ts2 in range(NSUB):
                    inst = e.matmul(
                        nb.t[:, ts2 * HN + 2 * dc: ts2 * HN + 2 * dc + 2],
                        ksq[:, ts2 * 128:(ts2 + 1) * 128], ind2[:],
                        start=(dc == 0 and ts2 == 0),
                        stop=(dc == DCH - 1 and ts2 == NSUB - 1))
                return inst
            step("tensor", nmm, waits=[ac_sq, gp_init], incs=[te.inc()])
        nb.ready = (te.at(),)
        consume("scalar", nb, (lambda e: e.activation(
            norms_s[:, 0:NSUB * HN], nb.t[:, 0:NSUB * HN], AF.Exp,
            scale=-0.5, bias=consts[:, 1:2])), waits=[norms_rd])
        return ac.at()

    kv_te_mark = [(te.h, 0), (te.h, 0)]
    act_fm_rd_mark = (te.h, 0)
    norms_rd_mark = (ve.h, 0)
    for it in range(1 if debug == 'a1' else NT):
        ac_kfm = qk_fm_tile(it, wT_a, ac_w, act_fm_rd_mark)
        ve_prev = ve.at()

        for ts in range(NSUB):
            for dhf in range(2):
                bb = banks[2 + ((ts * 2 + dhf) % 2)]
                def vmm(e, ts=ts, dhf=dhf, bb=bb):
                    for fc in range(FCH):
                        inst = e.matmul(
                            bb.t[:],
                            xfm_t[:, fc * TTILE + ts * 128: fc * TTILE + (ts + 1) * 128],
                            wT_b[:, fc * D + dhf * 512: fc * D + (dhf + 1) * 512],
                            start=(fc == 0), stop=(fc == FCH - 1))
                    return inst
                te_group(bb, vmm, waits=[ld.at(), ac_w])
                consume("vector", bb, (lambda e, ts=ts, dhf=dhf, bb=bb: e.tensor_copy(
                    vtm[:, ts * D + dhf * 512: ts * D + (dhf + 1) * 512], bb.t[:])),
                    waits=[ve_prev])
        ve_vtm = ve.at()
        xfm_rd_mark[0] = te.at()

        ac_st = norms_tile(ac_kfm, norms_rd_mark)

        for h in range(HN):
            dc, j, p = h // 2, h % 2, h % 2
            proj = proj2[:, p * 1024:(p + 1) * 1024]
            r_f = r_f2[:, p * 1024:(p + 1) * 1024]
            kf = kf2[:, p * 2048:(p + 1) * 2048]
            vs = vs2[:, p * 260:(p + 1) * 260]
            pb0, pb1 = banks[5], banks[7]
            def pmm(e, dc=dc, j=j, pb=None, base=0):
                for k in range(2):
                    ts2 = base + k
                    inst = e.matmul(
                        pb.t[:, k * 256:(k + 1) * 256],
                        act_fm[j * 64:(j + 1) * 64,
                               dc * TTILE + ts2 * 128: dc * TTILE + (ts2 + 1) * 128],
                        rfT[j * 64:(j + 1) * 64, :],
                        start=(k == 0), stop=(k == 1))
                return inst
            te_group(pb0, (lambda e, dc=dc, j=j, pb0=pb0: pmm(e, dc, j, pb0, 0)),
                     waits=[ac_kfm, ve_rfT])
            te_group(pb1, (lambda e, dc=dc, j=j, pb1=pb1: pmm(e, dc, j, pb1, 2)),
                     waits=[ac_kfm, ve_rfT])
            consume("scalar", pb0, (lambda e, proj=proj, pb0=pb0: e.copy(
                proj[:, 0:512], pb0.t[:])))
            consume("scalar", pb1, (lambda e, proj=proj, pb1=pb1: e.copy(
                proj[:, 512:1024], pb1.t[:])))
            ac_proj = ac.at()

            def trig(e, proj=proj, r_f=r_f):
                e.tensor_scalar(u_i32[:], proj[:], INV2PI, 256.0, ALU.mult, ALU.add)
                e.tensor_scalar(u_i32[:], u_i32[:], -256.0, None, ALU.add)
                e.scalar_tensor_tensor(r_f[:], u_i32[:], NEG2PI, proj[:],
                                       ALU.mult, ALU.add)
                e.tensor_scalar(t_f[:], r_f[:], PI_HI, None, ALU.is_gt)
                e.scalar_tensor_tensor(proj[:], t_f[:], NEG2PI, r_f[:],
                                       ALU.mult, ALU.add)
                e.tensor_scalar(r_f[:], proj[:], PI_LO, -PI_LO, ALU.min, ALU.max)
                e.tensor_scalar(t_f[:], r_f[:], PIHALF, -0.25, ALU.is_gt, ALU.add)
                e.scalar_tensor_tensor(proj[:], t_f[:], NEG2PI, r_f[:],
                                       ALU.mult, ALU.add)
                return e.tensor_scalar(proj[:], proj[:], PI_LO, -PI_LO,
                                       ALU.min, ALU.max)
            step("vector", trig, waits=[ac_proj], incs=[ve.inc()])
            ve_trig = ve.at()

            kf4 = kf.rearrange("p (ts m) -> p ts m", m=MM)
            r4 = r_f.rearrange("p (ts m) -> p ts m", m=M)
            c4 = proj.rearrange("p (ts m) -> p ts m", m=M)
            def sins(e, kf4=kf4, r4=r4, c4=c4):
                e.activation(kf4[:, :, M:MM], r4, AF.Sin)
                return e.activation(kf4[:, :, 0:M], c4, AF.Sin)
            step("scalar", sins, waits=[ve_trig, kv_te_mark[p]], incs=[ac.inc()])
            ac_kf = ac.at()

            def vsfn(e, h=h, vs=vs):
                for ts2 in range(NSUB):
                    sap = norms_s[:, ts2 * HN + h: ts2 * HN + h + 1]
                    e.tensor_scalar(vs[:, ts2 * 65: ts2 * 65 + 64],
                                    vtm[:, ts2 * D + h * 64: ts2 * D + (h + 1) * 64],
                                    sap, None, ALU.mult)
                    inst = e.tensor_copy(vs[:, ts2 * 65 + 64: ts2 * 65 + 65], sap)
                return inst
            step("vector", vsfn, waits=[ve_vtm, ac_st, kv_te_mark[p]],
                 incs=[ve.inc()])
            ve_vs = ve.at()

            kb = banks[6] if p == 0 else banks[4]
            def kvmm(e, kb=kb, kf=kf, vs=vs):
                first = True
                for ts2 in range(NSUB):
                    for ch in range(4):
                        inst = e.matmul(
                            kb.t[:, ch * 65:(ch + 1) * 65],
                            kf[:, ts2 * MM + ch * 128: ts2 * MM + (ch + 1) * 128],
                            vs[:, ts2 * 65:(ts2 + 1) * 65],
                            start=first, stop=(ts2 == NSUB - 1 and ch == 3))
                        first = False
                return inst
            te_group(kb, kvmm, waits=[ac_kf, ve_vs])
            kv_te_mark[p] = te.at()
            consume("vector", kb, (lambda e, h=h, kb=kb: e.tensor_tensor(
                kv[:, h * 260:(h + 1) * 260], kv[:, h * 260:(h + 1) * 260],
                kb.t[:, 0:260], ALU.add)), waits=[ve_init])
        act_fm_rd_mark = te.at()
        norms_rd_mark = ve.at()

    ve_kv = ve.at()

    if debug in ("a", "a1"):
        d_kv = dbg("d_kv", [128, HN * 4 * 65])
        d_ns = dbg("d_ns", [128, NSUB * HN])
        d_kf = dbg("d_kf", [128, 2048])
        step("gpsimd", lambda e: e.dma_start(out=d_kv, in_=kv[:]),
             waits=[ve_kv], incs=[wr.inc(16)])
        step("gpsimd", lambda e: e.dma_start(out=d_ns, in_=norms_s[:]),
             waits=[ve_kv], incs=[wr.inc(16)])
        step("gpsimd", lambda e: e.dma_start(out=d_kf, in_=kf2[:, 2048:4096]),
             waits=[ve_kv, ac.at()], incs=[wr.inc(16)])
        step("gpsimd", lambda e: e.nop(), waits=[wr.at()])
        emit(nc, steps)
        return nc, dbg_outs

    weight_transpose(wq_d, wT_a)
    weight_transpose(wo_d, wT_b)
    ac_w2 = ac.at()
    te_wt2 = te.at()

    qkv_te_mark = [(te.h, 0), (te.h, 0)]
    qkvs_rd_mark = (ve.h, 0)
    xtm_rd_mark = [te_wt2]
    for it in range(NT):
        t0 = it * TTILE
        ac_qfm = qk_fm_tile(it, wT_a, ac_w2, act_fm_rd_mark)
        ve_prev = ve.at()

        xsrc = x_d.rearrange("(nt ts p) d -> nt p ts d", ts=4, p=128)[it]
        step("sync", lambda e, xsrc=xsrc: e.dma_start(
            out=xtm_t.rearrange("p (ts d) -> p ts d", d=D),
            in_=xsrc[:, 0:2]), waits=[xtm_rd_mark[0]], incs=[ld.inc(16)])
        ld_xtm_halves = [ld.at(), None]
        xfm_rd_mark[0] = te.at()

        ac_sq_t = norms_tile(ac_qfm, norms_rd_mark)

        for h in range(HN):
            dc, j, p = h // 2, h % 2, h % 2
            proj = proj2[:, p * 1024:(p + 1) * 1024]
            r_f = r_f2[:, p * 1024:(p + 1) * 1024]
            kf = kf2[:, p * 2048:(p + 1) * 2048]
            pb0, pb1 = (banks[5], banks[7]) if p == 0 else (banks[2], banks[3])
            def pmm0(e, dc=dc, j=j, pb0=pb0):
                return e.matmul(pb0.t[:], rfT[j * 64:(j + 1) * 64, 0:128],
                                act_fm[j * 64:(j + 1) * 64, dc * TTILE:(dc + 1) * TTILE],
                                start=True, stop=True)
            def pmm1(e, dc=dc, j=j, pb1=pb1):
                return e.matmul(pb1.t[:], rfT[j * 64:(j + 1) * 64, 128:256],
                                act_fm[j * 64:(j + 1) * 64, dc * TTILE:(dc + 1) * TTILE],
                                start=True, stop=True)
            te_group(pb0, pmm0, waits=[ac_qfm, ve_rfT])
            te_group(pb1, pmm1, waits=[ac_qfm, ve_rfT])
            consume("scalar", pb0, (lambda e, proj=proj, pb0=pb0: e.copy(
                proj[:, 0:512], pb0.t[:])))
            consume("scalar", pb1, (lambda e, proj=proj, pb1=pb1: e.copy(
                proj[:, 512:1024], pb1.t[:])))
            ac_proj = ac.at()

            def trig(e, proj=proj, r_f=r_f):
                e.tensor_scalar(u_i32[:], proj[:], INV2PI, 256.0, ALU.mult, ALU.add)
                e.tensor_scalar(u_i32[:], u_i32[:], -256.0, None, ALU.add)
                e.scalar_tensor_tensor(r_f[:], u_i32[:], NEG2PI, proj[:],
                                       ALU.mult, ALU.add)
                e.tensor_scalar(t_f[:], r_f[:], PI_HI, None, ALU.is_gt)
                e.scalar_tensor_tensor(proj[:], t_f[:], NEG2PI, r_f[:],
                                       ALU.mult, ALU.add)
                e.tensor_scalar(r_f[:], proj[:], PI_LO, -PI_LO, ALU.min, ALU.max)
                e.tensor_scalar(t_f[:], r_f[:], PIHALF, -0.25, ALU.is_gt, ALU.add)
                e.scalar_tensor_tensor(proj[:], t_f[:], NEG2PI, r_f[:],
                                       ALU.mult, ALU.add)
                return e.tensor_scalar(proj[:], proj[:], PI_LO, -PI_LO,
                                       ALU.min, ALU.max)
            step("vector", trig, waits=[ac_proj], incs=[ve.inc()])
            ve_trig = ve.at()
            def sins(e, kf=kf, r_f=r_f, proj=proj):
                e.activation(kf[:, 2 * 512:4 * 512], r_f[:], AF.Sin)
                return e.activation(kf[:, 0:2 * 512], proj[:], AF.Sin)
            step("scalar", sins, waits=[ve_trig, qkv_te_mark[p]], incs=[ac.inc()])
            ac_qf = ac.at()

            qb = banks[6] if p == 0 else banks[4]
            for ts2 in range(NSUB):
                def qkvmm(e, h=h, ts2=ts2, qb=qb, kf=kf):
                    for ch in range(4):
                        inst = e.matmul(
                            qb.t[:, ts2 * 65:(ts2 + 1) * 65],
                            kf[:, ch * TTILE + ts2 * 128: ch * TTILE + (ts2 + 1) * 128],
                            kv[:, h * 260 + ch * 65: h * 260 + (ch + 1) * 65],
                            start=(ch == 0), stop=(ch == 3))
                    return inst
                te_group(qb, qkvmm, waits=[ac_qf, ve_kv])
            qkv_te_mark[p] = te.at()
            consume("scalar", qb, (lambda e, h=h, qb=qb: e.copy(
                qkvs.rearrange("p (s c) -> p s c", c=65)[:, h::HN, :],
                qb.t[:, 0:260].rearrange("p (s c) -> p s c", c=65))),
                waits=[qkvs_rd_mark])
        ac_q65_all = ac.at()

        qkr = qkvs.rearrange("p (s c) -> p s c", c=65)
        for ts2 in range(NSUB):
            zcols = qkr[:, ts2 * HN:(ts2 + 1) * HN, 64:65]
            sq_sl = norms_s[:, ts2 * HN:(ts2 + 1) * HN]
            chain("vector", [
                (lambda e, zcols=zcols, sq_sl=sq_sl: e.tensor_tensor(
                    zf[:, 0:HN], zcols, sq_sl, ALU.mult)),
                (lambda e: e.tensor_scalar(zf[:, HN:2 * HN], zf[:, 0:HN],
                                           1e-6, None, ALU.max)),
                (lambda e: e.reciprocal(zf[:, 0:HN], zf[:, HN:2 * HN])),
                (lambda e, sq_sl=sq_sl: e.tensor_tensor(
                    zf[:, HN:2 * HN], zf[:, 0:HN], sq_sl, ALU.mult)),
            ], waits=[ac_q65_all, ac_sq_t])
            ve_zf = ve.at()
            for h in range(HN):
                step("vector", (lambda e, h=h, ts2=ts2: e.tensor_scalar(
                    vtm[:, ts2 * D + h * 64: ts2 * D + (h + 1) * 64],
                    qkvs[:, (ts2 * HN + h) * 65: (ts2 * HN + h) * 65 + 64],
                    zf[:, HN + h: HN + h + 1], None, ALU.mult)),
                    waits=([ve_zf] if h == 0 else ()), incs=[ve.inc()])
        ve_attn = ve.at()
        qkvs_rd_mark = ve_attn

        for ts2 in range(NSUB):
            for dc in range(DCH):
                bb = banks[dc % 2]
                te_group(bb, (lambda e, ts2=ts2, dc=dc, bb=bb: e.matmul(
                    bb.t[:, 0:128], vtm[:, ts2 * D + dc * 128: ts2 * D + (dc + 1) * 128],
                    eye[:], is_transpose=True, start=True, stop=True)),
                    waits=[ve_attn, gp_init])
                consume("scalar", bb, (lambda e, ts2=ts2, dc=dc, bb=bb: e.copy(
                    act_fm[:, dc * TTILE + ts2 * 128: dc * TTILE + (ts2 + 1) * 128],
                    bb.t[:, 0:128])))
        ac_at = ac.at()

        for ts2 in range(NSUB):
            for dhf in range(2):
                bb = banks[2 + dhf]
                def omm(e, ts2=ts2, dhf=dhf, bb=bb):
                    for dc in range(DCH):
                        inst = e.matmul(
                            bb.t[:],
                            act_fm[:, dc * TTILE + ts2 * 128: dc * TTILE + (ts2 + 1) * 128],
                            wT_b[:, dc * D + dhf * 512: dc * D + (dhf + 1) * 512],
                            start=(dc == 0), stop=(dc == DCH - 1))
                    return inst
                te_group(bb, omm, waits=[ac_at, ac_w2])
                consume("vector", bb, (lambda e, ts2=ts2, dhf=dhf, bb=bb: e.tensor_tensor(
                    vtm[:, ts2 * D + dhf * 512: ts2 * D + (dhf + 1) * 512],
                    bb.t[:], xtm_t[:, (ts2 % 2) * D + dhf * 512:
                                   (ts2 % 2) * D + (dhf + 1) * 512],
                    ALU.add)), waits=[ld_xtm_halves[ts2 // 2]])
            if ts2 == 1:
                step("sync", (lambda e, xsrc=xsrc: e.dma_start(
                    out=xtm_t.rearrange("p (ts d) -> p ts d", d=D),
                    in_=xsrc[:, 2:4])), waits=[ve.at()], incs=[ld.inc(16)])
                ld_xtm_halves[1] = ld.at()
            ve_z = ve.at()

            zsl = vtm[:, ts2 * D:(ts2 + 1) * D]
            ve_y1 = ln_emit(zsl, stage[:], ve_z, track_amax=True)
            step("gpsimd", (lambda e, it=it, ts2=ts2: e.dma_start(
                out=y1_d[it * TTILE + ts2 * 128: it * TTILE + (ts2 + 1) * 128, :],
                in_=stage[:])), waits=[ve_y1], incs=[wr.inc(16)])
            wr_y1 = wr.at()
            step("scalar", lambda e: e.nop(), waits=[wr_y1])
        xtm_rd_mark[0] = ve.at()

    wr_y1_all = wr.at()
    ve_amax = ve.at()

    if debug == "b":
        d_y1 = dbg("d_y1", [T, D])
        d_amax = dbg("d_amax", [128, 1])
        d_ns2 = dbg("d_ns2", [128, NSUB * HN])
        d_kf2 = dbg("d_kf2", [128, 2048])
        d_qkvs = dbg("d_qkvs", [128, NSUB * HN * 65])
        d_attnfm = dbg("d_attnfm", [128, DCH * TTILE])
        d_zf = dbg("d_zf", [128, HN])
        step("gpsimd", lambda e: e.dma_start(out=d_y1, in_=y1_d),
             waits=[wr_y1_all], incs=[wr.inc(16)])
        step("gpsimd", lambda e: e.dma_start(out=d_amax, in_=amax_run[:]),
             waits=[ve_amax], incs=[wr.inc(16)])
        step("gpsimd", lambda e: e.dma_start(out=d_ns2, in_=norms_s[:]),
             waits=[ve_amax], incs=[wr.inc(16)])
        step("gpsimd", lambda e: e.dma_start(out=d_kf2, in_=kf2[:, 2048:4096]),
             waits=[ve_amax], incs=[wr.inc(16)])
        step("gpsimd", lambda e: e.dma_start(out=d_qkvs, in_=qkvs[:]),
             waits=[ve_amax], incs=[wr.inc(16)])
        step("gpsimd", lambda e: e.dma_start(out=d_attnfm, in_=act_fm[:]),
             waits=[ve_amax], incs=[wr.inc(16)])
        step("gpsimd", lambda e: e.dma_start(out=d_zf, in_=zf[:, 0:HN]),
             waits=[ve_amax], incs=[wr.inc(16)])
        step("gpsimd", lambda e: e.nop(), waits=[wr.at()])
        emit(nc, steps)
        return nc, dbg_outs

    b = banks[0]
    te_group(b, lambda e: e.matmul(b.t[0:1, 0:128], amax_run[:], eye[:],
                                   is_transpose=True, start=True, stop=True),
             waits=[ve_amax, gp_init])
    consume("scalar", b, lambda e: e.copy(sc128[0:1, 0:128], b.t[0:1, 0:128]))
    ac_sc = ac.at()
    step("gpsimd", lambda e: e.dma_start(out=cc1_in, in_=sc128[0:1, 0:128]),
         waits=[ac_sc], incs=[wr.inc(16)])
    step("gpsimd", lambda e: e.collective_compute(
        "AllReduce", ALU.max, replica_groups=[list(range(ncores))],
        ins=[cc1_in], outs=[cc1_out]), waits=[wr.at()], incs=[cc.inc()])
    step("sync", lambda e: e.dma_start(out=sc128[0:1, 0:128], in_=cc1_out),
         waits=[cc.at(), ac_sc], incs=[ld.inc(16)])
    ld_cc1 = ld.at()
    chain("vector", [
        (lambda e: e.tensor_reduce(smalls[0:1, 2:3], sc128[0:1, 0:128], AX.X, ALU.max)),
        (lambda e: e.tensor_tensor(smalls[0:1, 3:4], smalls[0:1, 2:3],
                                   smalls[0:1, 0:1], ALU.mult)),
        (lambda e: e.tensor_scalar(smalls[0:1, 3:4], smalls[0:1, 3:4],
                                   INV127, None, ALU.mult)),
        (lambda e: e.reciprocal(smalls[0:1, 4:5], smalls[0:1, 3:4])),
    ], waits=[ld_cc1, ac.at()])
    ve_sx = ve.at()
    b = banks[1]
    te_group(b, lambda e: e.matmul(b.t[:, 0:2], ones_mat[0:1, 0:128],
                                   smalls[0:1, 3:5], start=True, stop=True),
             waits=[ve_sx, ve_init])
    consume("vector", b, lambda e: e.tensor_copy(smalls[:, 3:5], b.t[:, 0:2]))
    ve_sxb = ve.at()

    bar = [te.at(), ve.at(), ac.at(), gp.at(), ld.at(), wr.at(), xb.at()]
    step("tensor", lambda e: e.nop(), waits=bar)
    step("vector", lambda e: e.nop(), waits=bar)
    step("scalar", lambda e: e.nop(), waits=bar)
    step("gpsimd", lambda e: e.nop(), waits=bar)
    step("sync", lambda e: e.nop(), waits=bar)

    for cm in (cm_zf, cm_qkvs, cm_kv, cm_vs, cm_ns, cm_kf, cm_rf,
               cm_tf, cm_ui, cm_proj, cm_vtm, cm_actfm, cm_xtmt, cm_xfmt,
               cm_wTb, cm_wTa, cm_rfT):
        cm.__exit__(None, None, None)

    wld, _ = sbuf("wld", [128, FI], F32)
    i8b, _ = sbuf("i8b", [128, FI], I8)
    bfb, _ = sbuf("bfb", [128, FI], BF16)
    y1t, _ = sbuf("y1t", [128, D], F32)
    ixf, _ = sbuf("ixf", [128, HCH * TTILE], BF16)
    silu_t, _ = sbuf("silu_t", [128, TTILE], F32)
    up_t, _ = sbuf("up_t", [128, TTILE], F32)
    h_t, _ = sbuf("h_t", [128, TTILE], F32)
    sgu, _ = sbuf("sgu", [128, 96], F32)
    igT, cm_igT = sbuf("igT", [128, HCH * FI], BF16)
    iupT, cm_iupT = sbuf("iupT", [128, HCH * FI], BF16)

    step("vector", lambda e: e.memset(amax_run[:], 0.0), waits=[ve_sxb], incs=[ve.inc()])
    ve_am0 = ve.at()

    def wquant(w_ap, q_ap, scol, nchunk, width):
        for c_ in range(nchunk):
            step("sync", (lambda e, c_=c_, w_ap=w_ap, width=width: e.dma_start(
                out=wld[:, 0:width], in_=w_ap[c_ * 128:(c_ + 1) * 128, :])),
                waits=[ve.at(), gp.at()], incs=[ld.inc(16)])
            ld_w = ld.at()
            chain("vector", [
                (lambda e, width=width: e.tensor_reduce(
                    lnst[:, 8:9], wld[:, 0:width], AX.X, ALU.max,
                    apply_absolute_value=True)),
                (lambda e, c_=c_, scol=scol: e.tensor_scalar(
                    sgu[:, scol + c_: scol + c_ + 1], lnst[:, 8:9],
                    INV127, None, ALU.mult)),
                (lambda e: e.reciprocal(lnst[:, 9:10], lnst[:, 8:9])),
                (lambda e: e.tensor_scalar(lnst[:, 10:11], lnst[:, 9:10],
                                           127.0, None, ALU.mult)),
                (lambda e, width=width: e.tensor_scalar(
                    i8b[:, 0:width], wld[:, 0:width], lnst[:, 10:11],
                    None, ALU.mult)),
                (lambda e, width=width: e.tensor_copy(
                    bfb[:, 0:width], i8b[:, 0:width])),
            ], waits=[ld_w, wr.at()])
            ve_q = ve.at()
            step("gpsimd", (lambda e, c_=c_, q_ap=q_ap, width=width: e.dma_start(
                out=q_ap[c_ * 128:(c_ + 1) * 128, :], in_=bfb[:, 0:width])),
                waits=[ve_q], incs=[wr.inc(16)])

    wquant(wg_d, wgq_d, 0, ICH, D)
    wquant(wu_d, wuq_d, 22, ICH, D)
    wquant(wd_d, wdq_d, 44, HCH, FI)
    wr_wq = wr.at()

    step("vector", lambda e: (
        e.tensor_scalar(sgu[:, 52:74], sgu[:, 0:22], smalls[:, 3:4], None, ALU.mult),
        e.tensor_scalar(sgu[:, 74:96], sgu[:, 22:44], smalls[:, 3:4], None, ALU.mult))[-1],
        waits=[ve_sxb], incs=[ve.inc()])
    ve_sgx = ve.at()

    for hc in range(HCH):
        for (i0, ilen) in IBLOCKS:
            step("sync", (lambda e, hc=hc, i0=i0, ilen=ilen: e.dma_start(
                out=igT[:, hc * FI + i0: hc * FI + i0 + ilen],
                in_=wgq_d[i0:i0 + ilen, hc * 128:(hc + 1) * 128], transpose=True)),
                waits=[wr_wq], incs=[xb.inc(16)])
            step("sync", (lambda e, hc=hc, i0=i0, ilen=ilen: e.dma_start(
                out=iupT[:, hc * FI + i0: hc * FI + i0 + ilen],
                in_=wuq_d[i0:i0 + ilen, hc * 128:(hc + 1) * 128], transpose=True)),
                waits=[wr_wq], incs=[xb.inc(16)])
    xb_w = xb.at()

    for it in range(NT):
        t0 = it * TTILE
        for ts2 in range(NSUB):
            step("sync", (lambda e, t0=t0, ts2=ts2: e.dma_start(
                out=y1t[:], in_=y1_d[t0 + ts2 * 128: t0 + (ts2 + 1) * 128, :])),
                waits=[wr_y1_all, ve.at()], incs=[ld.inc(16)])
            ld_y1 = ld.at()
            step("vector", (lambda e: (
                e.tensor_scalar(y1t[:], y1t[:], smalls[:, 4:5], 127.0,
                                ALU.mult, ALU.min),
                e.tensor_scalar(i8b[:, 0:D], y1t[:], -128.0, None, ALU.max),
                e.tensor_copy(bfb[:, 0:D], i8b[:, 0:D]))[-1]),
                waits=[ld_y1, ve_sxb, wr.at()], incs=[ve.inc()])
            ve_ix = ve.at()
            step("gpsimd", (lambda e, t0=t0, ts2=ts2: e.dma_start(
                out=ixq_d[t0 + ts2 * 128: t0 + (ts2 + 1) * 128, :], in_=bfb[:, 0:D])),
                waits=[ve_ix], incs=[wr.inc(16)])
        wr_ix = wr.at()
        for hc in range(HCH):
            step("sync", (lambda e, t0=t0, hc=hc: e.dma_start(
                out=ixf[:, hc * TTILE:(hc + 1) * TTILE],
                in_=ixq_d[t0:t0 + TTILE, hc * 128:(hc + 1) * 128], transpose=True)),
                waits=[wr_ix, te.at()], incs=[xb.inc(16)])
        xb_ix = xb.at()

        for ic in range(ICH):
            gb = banks[0 + (ic % 2)]
            ub = banks[2 + (ic % 2)]
            def gmm(e, ic=ic, gb=gb):
                for hc in range(HCH):
                    inst = e.matmul(
                        gb.t[:], igT[:, hc * FI + ic * 128: hc * FI + (ic + 1) * 128],
                        ixf[:, hc * TTILE:(hc + 1) * TTILE],
                        start=(hc == 0), stop=(hc == HCH - 1))
                return inst
            def umm(e, ic=ic, ub=ub):
                for hc in range(HCH):
                    inst = e.matmul(
                        ub.t[:], iupT[:, hc * FI + ic * 128: hc * FI + (ic + 1) * 128],
                        ixf[:, hc * TTILE:(hc + 1) * TTILE],
                        start=(hc == 0), stop=(hc == HCH - 1))
                return inst
            te_group(gb, gmm, waits=[xb_ix, xb_w])
            te_group(ub, umm, waits=[xb_ix, xb_w])
            consume("scalar", gb, (lambda e, ic=ic, gb=gb: e.activation(
                silu_t[:], gb.t[:], AF.Silu, scale=sgu[:, 52 + ic: 53 + ic])),
                waits=[ve_sgx, ve.at()])
            ac_si = ac.at()
            consume("vector", ub, (lambda e, ic=ic, ub=ub: e.tensor_scalar(
                up_t[:], ub.t[:], sgu[:, 74 + ic: 75 + ic], None, ALU.mult)),
                waits=[ve_sgx, wr.at()])
            ve_up = ve.at()
            chain("vector", [
                (lambda e: e.tensor_tensor(h_t[:], silu_t[:], up_t[:], ALU.mult)),
                (lambda e: e.tensor_reduce(lnst[:, 12:13], h_t[:], AX.X, ALU.max,
                                           apply_absolute_value=True)),
                (lambda e: e.tensor_tensor(amax_run[:], amax_run[:],
                                           lnst[:, 12:13], ALU.max)),
            ], waits=[ac_si, ve_up, ve_am0, wr.at()])
            ve_h = ve.at()
            step("gpsimd", (lambda e, t0=t0, ic=ic: e.dma_start(
                out=h_dd[ic * 128:(ic + 1) * 128, t0:t0 + TTILE], in_=h_t[:])),
                waits=[ve_h], incs=[wr.inc(16)])
    ve_amh = ve.at()
    wr_h = wr.at()

    b = banks[4]
    te_group(b, lambda e: e.matmul(b.t[0:1, 0:128], amax_run[:], eye[:],
                                   is_transpose=True, start=True, stop=True),
             waits=[ve_amh, gp_init])
    consume("scalar", b, lambda e: e.copy(sc128[0:1, 0:128], b.t[0:1, 0:128]),
            waits=[ld.at()])
    ac_sc2 = ac.at()
    step("gpsimd", lambda e: e.dma_start(out=cc2_in, in_=sc128[0:1, 0:128]),
         waits=[ac_sc2], incs=[wr.inc(16)])
    step("gpsimd", lambda e: e.collective_compute(
        "AllReduce", ALU.max, replica_groups=[list(range(ncores))],
        ins=[cc2_in], outs=[cc2_out]), waits=[wr.at()], incs=[cc.inc()])
    step("sync", lambda e: e.dma_start(out=sc128[0:1, 0:128], in_=cc2_out),
         waits=[cc.at(), ac_sc2], incs=[ld.inc(16)])
    ld_cc2 = ld.at()
    chain("vector", [
        (lambda e: e.tensor_reduce(smalls[0:1, 5:6], sc128[0:1, 0:128], AX.X, ALU.max)),
        (lambda e: e.tensor_tensor(smalls[0:1, 6:7], smalls[0:1, 5:6],
                                   smalls[0:1, 1:2], ALU.mult)),
        (lambda e: e.tensor_scalar(smalls[0:1, 6:7], smalls[0:1, 6:7],
                                   INV127, None, ALU.mult)),
        (lambda e: e.reciprocal(smalls[0:1, 7:8], smalls[0:1, 6:7])),
    ], waits=[ld_cc2, ac.at()])
    ve_sh = ve.at()
    b = banks[5]
    te_group(b, lambda e: e.matmul(b.t[:, 0:2], ones_mat[0:1, 0:128],
                                   smalls[0:1, 6:8], start=True, stop=True),
             waits=[ve_sh, ve_init])
    consume("vector", b, lambda e: e.tensor_copy(smalls[:, 6:8], b.t[:, 0:2]))
    ve_shb = ve.at()

    cm_iupT.__exit__(None, None, None)
    cm_igT.__exit__(None, None, None)
    idT, _ = sbuf("idT", [128, ICH * D], BF16)
    ihf, _ = sbuf("ihf", [128, ICH * TTILE], BF16)
    hld, _ = sbuf("hld", [128, TTILE], F32)
    sdx, _ = sbuf("sdx", [128, D], F32)
    zb, _ = sbuf("zb", [128, D], F32)

    for ic in range(ICH):
        for hb in range(2):
            step("sync", (lambda e, ic=ic, hb=hb: e.dma_start(
                out=idT[:, ic * D + hb * 512: ic * D + (hb + 1) * 512],
                in_=wdq_d[hb * 512:(hb + 1) * 512, ic * 128:(ic + 1) * 128],
                transpose=True)), waits=[wr_wq, te.at()], incs=[xb.inc(16)])
    xb_d = xb.at()

    sd_d = nc.dram_tensor("sdd", [128, 8], F32, kind="Internal").ap()
    step("gpsimd", lambda e: e.dma_start(out=sd_d, in_=sgu[:, 44:52]),
         waits=[ve.at()], incs=[wr.inc(16)])
    wr_sd = wr.at()
    sd_row = stage[0:1, 0:1024].rearrange("p (hb jj) -> p hb jj", jj=128)
    step("sync", lambda e: e.dma_start(
        out=sd_row, in_=sd_d.rearrange("jj hb -> hb jj")[None],
        allow_slow_non_contiguous=True),
        waits=[wr_sd, ac.at(), ve.at()], incs=[ld.inc(16)])
    ld_sdr = ld.at()
    for nh in range(2):
        b = banks[6 + nh]
        te_group(b, (lambda e, nh=nh, b=b: e.matmul(
            b.t[:], ones_mat[0:1, 0:128], stage[0:1, nh * 512:(nh + 1) * 512],
            start=True, stop=True)), waits=[ld_sdr, ve_init])
        consume("vector", b, (lambda e, nh=nh, b=b: e.tensor_copy(
            sdx[:, nh * 512:(nh + 1) * 512], b.t[:])))
    step("vector", lambda e: e.tensor_scalar(sdx[:], sdx[:], smalls[:, 6:7],
                                             None, ALU.mult),
         waits=[ve_shb], incs=[ve.inc()])
    ve_sdx = ve.at()

    for it in range(NT):
        t0 = it * TTILE
        for ic in range(ICH):
            step("sync", (lambda e, t0=t0, ic=ic: e.dma_start(
                out=hld[:], in_=h_dd[ic * 128:(ic + 1) * 128, t0:t0 + TTILE])),
                waits=[wr_h, ve.at()], incs=[ld.inc(16)])
            ld_h = ld.at()
            step("vector", (lambda e, ic=ic: (
                e.tensor_scalar(hld[:], hld[:], smalls[:, 7:8], 127.0,
                                ALU.mult, ALU.min),
                e.tensor_scalar(i8b[:, 0:TTILE], hld[:], -128.0, None, ALU.max),
                e.tensor_copy(ihf[:, ic * TTILE:(ic + 1) * TTILE], i8b[:, 0:TTILE]))[-1]),
                waits=[ld_h, ve_shb, te.at()], incs=[ve.inc()])
        ve_ih = ve.at()

        for ts2 in range(NSUB):
            for hb in range(2):
                bb = banks[0 + hb]
                def dmm(e, ts2=ts2, hb=hb, bb=bb):
                    for ic in range(ICH):
                        inst = e.matmul(
                            bb.t[:],
                            ihf[:, ic * TTILE + ts2 * 128: ic * TTILE + (ts2 + 1) * 128],
                            idT[:, ic * D + hb * 512: ic * D + (hb + 1) * 512],
                            start=(ic == 0), stop=(ic == ICH - 1))
                    return inst
                te_group(bb, dmm, waits=[ve_ih, xb_d])
            step("sync", (lambda e, t0=t0, ts2=ts2: e.dma_start(
                out=y1t[:], in_=y1_d[t0 + ts2 * 128: t0 + (ts2 + 1) * 128, :])),
                waits=[ve.at()], incs=[ld.inc(16)])
            ld_y1b = ld.at()
            for hb in range(2):
                bb = banks[0 + hb]
                consume("vector", bb, (lambda e, hb=hb, bb=bb: (
                    e.tensor_tensor(zb[:, hb * 512:(hb + 1) * 512], bb.t[:],
                                    sdx[:, hb * 512:(hb + 1) * 512], ALU.mult),
                    e.tensor_tensor(zb[:, hb * 512:(hb + 1) * 512],
                                    zb[:, hb * 512:(hb + 1) * 512],
                                    y1t[:, hb * 512:(hb + 1) * 512], ALU.add))[-1]),
                    waits=[ld_y1b, ve_sdx])
            ve_z2 = ve.at()

            ve_out = ln_emit(zb[:], stage[:], ve_z2, track_amax=False)
            step("gpsimd", (lambda e, t0=t0, ts2=ts2: e.dma_start(
                out=y_d[t0 + ts2 * 128: t0 + (ts2 + 1) * 128, :], in_=stage[:])),
                waits=[ve_out], incs=[wr.inc(16)])
            wr_o = wr.at()
            step("scalar", lambda e: e.nop(), waits=[wr_o])

    step("gpsimd", lambda e: e.nop(), waits=[wr.at()])
    emit(nc, steps)
    return nc, dbg_outs


def emit(nc, steps):
    with nc.Block() as block:
        for eng_name in ("sync", "tensor", "vector", "scalar", "gpsimd"):
            mine = [(w, f, i) for (e, w, f, i) in steps if e == eng_name]

            def run(eng, mine=mine):
                for waits, fn, incs in mine:
                    for (h, v) in waits:
                        eng.wait_ge(h, v)
                    inst = fn(eng)
                    for (h, k) in incs:
                        inst = inst.then_inc(h, k)
            getattr(block, eng_name)(run)


def kernel(L0_attn_x, L0_attn_wq, L0_attn_wk, L0_attn_wv, L0_attn_wo,
           L0_attn_rf, L0_ffn_clip_x, L0_ffn_clip_down, L0_ffn_w_gate,
           L0_ffn_w_up, L0_ffn_w_down):
    nc, _ = build()
    base = {
        "wq": np.ascontiguousarray(L0_attn_wq),
        "wk": np.ascontiguousarray(L0_attn_wk),
        "wv": np.ascontiguousarray(L0_attn_wv),
        "wo": np.ascontiguousarray(L0_attn_wo),
        "rf": np.ascontiguousarray(L0_attn_rf),
        "clip_x": np.ascontiguousarray(L0_ffn_clip_x),
        "clip_down": np.ascontiguousarray(L0_ffn_clip_down),
        "w_gate": np.ascontiguousarray(L0_ffn_w_gate),
        "w_up": np.ascontiguousarray(L0_ffn_w_up),
        "w_down": np.ascontiguousarray(L0_ffn_w_down),
    }
    in_maps = [dict(base, x=np.ascontiguousarray(L0_attn_x[c])) for c in range(8)]
    res = run_bass_kernel_spmd(nc, in_maps, core_ids=list(range(8)))
    return np.stack([res.results[c]["y"] for c in range(8)], axis=0)
